# version 01
# baseline (speedup 1.0000x reference)
import contextlib
import math

import numpy as np
import ml_dtypes

import concourse.bass as bass
import concourse.mybir as mybir
from concourse.bass import AP
from concourse.bass_utils import run_bass_kernel_spmd

F32 = mybir.dt.float32
BF16 = mybir.dt.bfloat16
AX = mybir.AxisListType.X
AF = mybir.ActivationFunctionType
OPT = mybir.AluOpType
BF = ml_dtypes.bfloat16

NT = 4096
T = 2048
D = 1024
NH = 16
DH = 64
NEG = -1.0e30

ENGS = ("tensor", "vector", "scalar", "sync", "gpsimd")


class Pr:

    def __init__(self):
        self.ops = {e: [] for e in ENGS}
        self.cnt = {e: 0 for e in ENGS}
        self.lastw = {}
        self.lastr = {}
        self.seen = {e: {e2: 0 for e2 in ENGS} for e in ENGS}

    def _wait(self, eng, pe, pc):
        if pc <= 0:
            return
        if pe == eng and eng in ("tensor", "sync", "gpsimd"):
            return
        if self.seen[eng][pe] >= pc:
            return
        self.seen[eng][pe] = pc
        self.ops[eng].append(("w", pe, pc))

    def op(self, eng, fn, reads=(), writes=(), inc=1):
        deps = []
        for r in reads:
            if r in self.lastw:
                deps.append(self.lastw[r])
        for w in writes:
            if w in self.lastw:
                deps.append(self.lastw[w])
            for pe, pc in self.lastr.get(w, {}).items():
                deps.append((pe, pc))
        for pe, pc in deps:
            self._wait(eng, pe, pc)
        self.ops[eng].append(("i", fn, inc))
        self.cnt[eng] += inc
        c = self.cnt[eng]
        for r in reads:
            self.lastr.setdefault(r, {})[eng] = c
        for w in writes:
            self.lastw[w] = (eng, c)
            self.lastr[w] = {}

    def dma(self, eng, fn, reads=(), writes=()):
        self.op(eng, fn, reads, writes, inc=16)

    def finalize(self, nc):
        with contextlib.ExitStack() as st:
            sems = {e: st.enter_context(nc.semaphore("sem_" + e)) for e in ENGS}
            blk = st.enter_context(nc.Block())
            for e in ENGS:
                items = self.ops[e]

                def body(eng, items=items, e=e):
                    for it in items:
                        if it[0] == "w":
                            eng.wait_ge(sems[it[1]], it[2])
                        else:
                            it[1](eng).then_inc(sems[e], it[2])

                getattr(blk, e)(body)


def _alt(buflist):
    i = [0]

    def nxt():
        b = buflist[i[0] % len(buflist)]
        i[0] += 1
        return b
    return nxt


def emit_ln(P, xt, rxt, xm, rxm, st, rst, gb=None, bb=None, rgb=None, rbb=None,
            out=None, rout=None):
    if out is None:
        out, rout = xm, rxm
    P.op("vector", lambda e, xt=xt, st=st: e.reduce_sum(st[:, 0:1], xt, AX),
         reads=[rxt], writes=[rst])
    P.op("vector", lambda e, st=st: e.tensor_scalar_mul(st[:, 1:2], st[:, 0:1], 1.0 / D),
         reads=[rst], writes=[rst])
    P.op("vector", lambda e, xt=xt, xm=xm, st=st:
         e.tensor_single_scalar(xm, xt, st[:, 1:2], OPT.subtract),
         reads=[rxt, rst], writes=[rxm])
    P.op("scalar", lambda e, xt=xt, xm=xm: e.square(xt, xm),
         reads=[rxm], writes=[rxt])
    P.op("vector", lambda e, xt=xt, st=st: e.reduce_sum(st[:, 2:3], xt, AX),
         reads=[rxt], writes=[rst])
    P.op("vector", lambda e, st=st:
         e.tensor_scalar(st[:, 3:4], st[:, 2:3], 1.0 / D, 1e-5, OPT.mult, OPT.add),
         reads=[rst], writes=[rst])
    P.op("scalar", lambda e, st=st: e.activation(st[:, 4:5], st[:, 3:4], AF.Sqrt),
         reads=[rst], writes=[rst])
    P.op("vector", lambda e, st=st: e.reciprocal(st[:, 5:6], st[:, 4:5]),
         reads=[rst], writes=[rst])
    P.op("vector", lambda e, xm=xm, st=st, out=out:
         e.tensor_single_scalar(out, xm, st[:, 5:6], OPT.mult),
         reads=[rxm, rst], writes=[rout])
    if gb is not None:
        P.op("vector", lambda e, out=out, gb=gb: e.tensor_mul(out, out, gb),
             reads=[rout, rgb], writes=[rout])
        P.op("vector", lambda e, out=out, bb=bb: e.tensor_add(out, out, bb),
             reads=[rout, rbb], writes=[rout])


def emit_load_bf16(P, dram, dsts, stages, rstages, tag, n_free, chunk=1024):
    nxt = _alt(list(range(len(stages))))
    for j, dst in enumerate(dsts):
        for c0 in range(0, n_free, chunk):
            si = nxt()
            stg, rstg = stages[si], rstages[si]
            w = min(chunk, n_free - c0)
            P.dma("sync", lambda e, stg=stg, w=w, j=j, c0=c0:
                  e.dma_start(stg[:, 0:w], dram[j * 128:(j + 1) * 128, c0:c0 + w]),
                  reads=[], writes=[rstg])
            P.op("scalar", lambda e, stg=stg, w=w, dst=dst, c0=c0:
                 e.copy(dst[:, c0:c0 + w], stg[:, 0:w]),
                 reads=[rstg], writes=[tag + str(j)])


def emit_bcast_row(P, psum, rpsum, ones, rones, row_ap, rrow, dst, rdst, n, f32=True):
    step = 512 if f32 else 1024
    for c0 in range(0, n, step):
        w = min(step, n - c0)
        P.op("tensor", lambda e, c0=c0, w=w:
             e.matmul(psum[:, c0:c0 + w], ones, row_ap[:, c0:c0 + w],
                      start=True, stop=True),
             reads=[rones, rrow], writes=[rpsum])
    P.op("vector", lambda e: e.tensor_copy(dst, psum[:, 0:n]),
         reads=[rpsum], writes=[rdst])


def _bucket(n):
    if n < 16:
        return n
    v = 16 + int(math.log(n / 16.0) / math.log(8.0) * 16.0)
    return min(v, 31)


def build_attn(flavor):
    nc = bass.Bass("TRN2", target_bir_lowering=False)
    causal = flavor in ("l0a", "l0b", "l1b")
    sink = flavor == "l0a"
    qkvbias = flavor == "l0a"
    rope = flavor == "l1a"
    t5 = flavor == "l1b"

    x_in = nc.dram_tensor("x_in", [NT, D], F32, kind="ExternalInput").ap()
    x_out = nc.dram_tensor("x_out", [NT, D], F32, kind="ExternalOutput").ap()
    wq_d = nc.dram_tensor("wq", [D, D], F32, kind="ExternalInput").ap()
    wk_d = nc.dram_tensor("wk", [D, D], F32, kind="ExternalInput").ap()
    wv_d = nc.dram_tensor("wv", [D, D], F32, kind="ExternalInput").ap()
    wo_d = nc.dram_tensor("wo", [D, D], F32, kind="ExternalInput").ap()
    bo_d = nc.dram_tensor("bo_row", [1, D], F32, kind="ExternalInput").ap()
    ones_d = nc.dram_tensor("c_ones", [1, 128], F32, kind="ExternalInput").ap()
    eye_d = nc.dram_tensor("c_eye_bf", [128, 128], BF16, kind="ExternalInput").ap()
    if causal:
        mask_d = nc.dram_tensor("c_mask", [128, 128], F32, kind="ExternalInput").ap()
    if qkvbias:
        bq_d = nc.dram_tensor("bq_col", [D, 1], F32, kind="ExternalInput").ap()
        bk_d = nc.dram_tensor("bk_col", [D, 1], F32, kind="ExternalInput").ap()
        bv_d = nc.dram_tensor("bv_row", [1, D], F32, kind="ExternalInput").ap()
        sink_d = nc.dram_tensor("sink_row", [1, NH], F32, kind="ExternalInput").ap()
    if rope:
        cos_d = nc.dram_tensor("cos_tm", [T, 32], F32, kind="ExternalInput").ap()
        sin_d = nc.dram_tensor("sin_tm", [T, 32], F32, kind="ExternalInput").ap()
    if t5:
        oh_d = nc.dram_tensor("oh_bf", [32, 2304], BF16, kind="ExternalInput").ap()
        jrev_d = nc.dram_tensor("j_bf", [128, 128], BF16, kind="ExternalInput").ap()
        rel_d = nc.dram_tensor("relemb", [32, NH], F32, kind="ExternalInput").ap()
        tab_h = nc.dram_tensor("tab_h", [NH, 2304], BF16, kind="Internal").ap()

    P = Pr()
    with contextlib.ExitStack() as st_:
        _nm = [0]

        def sb(shape, dt):
            _nm[0] += 1
            return st_.enter_context(nc.sbuf_tensor("sb%d" % _nm[0], shape, dt)).ap()

        def ps(shape, dt):
            _nm[0] += 1
            return st_.enter_context(nc.psum_tensor("ps%d" % _nm[0], shape, dt)).ap()

        yT = [sb([128, T], BF16) for _ in range(8)]
        zT = [sb([128, T], BF16) for _ in range(8)]
        wq = [sb([128, D], BF16) for _ in range(8)]
        wk = [sb([128, D], BF16) for _ in range(8)]
        wv = [sb([128, D], BF16) for _ in range(8)]
        wo = [sb([128, D], BF16) for _ in range(8)]
        QT = sb([128, T], BF16)
        KT = sb([128, T], BF16)
        Vs = sb([128, T], BF16)
        S = sb([128, T], F32)
        Pb = sb([128, T], BF16)
        PT = [sb([128, 128], BF16) for _ in range(2)]
        stg = [sb([128, 1024], F32) for _ in range(2)]
        xt = sb([128, D], F32)
        xm = sb([128, D], F32)
        yt = sb([128, D], BF16)
        xnew = [sb([128, D], F32) for _ in range(2)]
        bob = sb([128, D], F32)
        stt = sb([128, 8], F32)
        ones = sb([1, 128], F32)
        eye = sb([128, 128], BF16)
        mrow = sb([128, 2], F32)
        if causal:
            maskt = sb([128, 128], F32)
        if sink:
            sinkb = sb([128, NH], F32)
            bvb = sb([128, 128], F32)
            bvrow = sb([1, D], F32)
            bcol = sb([128, 2], F32)
        if rope:
            cosb = sb([128, 16 * 32], F32)
            sinb = sb([128, 16 * 32], F32)
            qtm = [sb([128, 128], F32) for _ in range(2)]
            rof = sb([128, 128], F32)
            robf = sb([128, 128], BF16)
            t32 = [sb([128, 32], F32) for _ in range(2)]
        if t5:
            ohb = sb([32, 2304], BF16)
            relf = sb([32, NH], F32)
            relb = sb([32, NH], BF16)
            tabf = sb([128, NH], F32)
            tabb = sb([128, NH], BF16)
            tabT = sb([16, 2304], BF16)
            jrev = sb([128, 128], BF16)
            win = [sb([128, 128], BF16) for _ in range(2)]
            bias_sb = sb([128, 16 * 128], F32)
        borow = sb([1, D], F32)

        psA = ps([128, 512], F32)
        psA2 = ps([128, 512], F32)
        psT = [ps([128, 128], BF16) for _ in range(2)]
        psZ = ps([64, 128], F32)
        psO = ps([128, 1024], F32)
        if t5:
            psB = ps([128, 128], F32)
            psR = psB[:, 0:16]

        P.dma("sync", lambda e: e.dma_start(ones, ones_d), writes=["ones"])
        P.dma("sync", lambda e: e.dma_start(eye, eye_d), writes=["eye"])
        P.dma("sync", lambda e: e.dma_start(borow, bo_d), writes=["borow"])
        if causal:
            P.dma("sync", lambda e: e.dma_start(maskt, mask_d), writes=["maskt"])
        emit_load_bf16(P, wq_d, wq, stg, ["stg0", "stg1"], "wq", D)
        emit_load_bf16(P, wk_d, wk, stg, ["stg0", "stg1"], "wk", D)
        emit_load_bf16(P, wv_d, wv, stg, ["stg0", "stg1"], "wv", D)
        emit_load_bf16(P, wo_d, wo, stg, ["stg0", "stg1"], "wo", D)
        emit_bcast_row(P, psO, "psO", ones[0:1, :], "ones", borow, "borow",
                       bob, "bob", D)
        if sink:
            P.dma("sync", lambda e: e.dma_start(bvrow, bv_d), writes=["bvrow"])
            sinkrow = sb([1, NH], F32)
            P.dma("sync", lambda e: e.dma_start(sinkrow, sink_d), writes=["sinkrow"])
            P.op("tensor", lambda e: e.matmul(psO[:, 0:NH], ones[0:1, :],
                                              sinkrow, start=True, stop=True),
                 reads=["ones", "sinkrow"], writes=["psO"])
            P.op("vector", lambda e: e.tensor_copy(sinkb, psO[:, 0:NH]),
                 reads=["psO"], writes=["sinkb"])
        if rope:
            for tt in range(16):
                P.dma("sync", lambda e, tt=tt: e.dma_start(
                    cosb[:, tt * 32:(tt + 1) * 32], cos_d[tt * 128:(tt + 1) * 128, :]),
                    writes=["cosb"])
                P.dma("sync", lambda e, tt=tt: e.dma_start(
                    sinb[:, tt * 32:(tt + 1) * 32], sin_d[tt * 128:(tt + 1) * 128, :]),
                    writes=["sinb"])
        if t5:
            eyef = sb([128, 128], F32)
            eyef_d = nc.dram_tensor("c_eye_f", [128, 128], F32, kind="ExternalInput").ap()
            P.dma("sync", lambda e: e.dma_start(eyef, eyef_d), writes=["eyef"])
            P.dma("sync", lambda e: e.dma_start(ohb, oh_d), writes=["ohb"])
            P.dma("sync", lambda e: e.dma_start(jrev, jrev_d), writes=["jrev"])
            P.dma("sync", lambda e: e.dma_start(relf, rel_d), writes=["relf"])
            P.op("scalar", lambda e: e.copy(relb, relf), reads=["relf"], writes=["relb"])
            for ut in range(18):
                P.op("tensor", lambda e, ut=ut: e.matmul(
                    psR, ohb[:, ut * 128:(ut + 1) * 128], relb, start=True, stop=True),
                    reads=["ohb", "relb"], writes=["psB"])
                P.op("scalar", lambda e: e.copy(tabb, psR),
                     reads=["psB"], writes=["tabb"])
                P.op("tensor", lambda e: e.transpose(psT[0][0:16, :], tabb, eye),
                     reads=["tabb", "eye"], writes=["psT0"])
                P.op("vector", lambda e, ut=ut: e.tensor_copy(
                    tabT[:, ut * 128:(ut + 1) * 128], psT[0][0:16, :]),
                    reads=["psT0"], writes=["tabT"])
            P.dma("gpsimd", lambda e: e.dma_start(tab_h, tabT),
                  reads=["tabT"], writes=["tab_h"])

        for b in range(2):
            for tt in range(16):
                r0 = b * T + tt * 128
                P.dma("sync", lambda e, r0=r0: e.dma_start(xt, x_in[r0:r0 + 128, :]),
                      writes=["xt"])
                emit_ln(P, xt, "xt", xm, "xm", stt, "stt")
                P.op("scalar", lambda e: e.copy(yt, xm), reads=["xm"], writes=["yt"])
                for j in range(8):
                    pt = psT[j % 2]
                    rpt = "psT%d" % (j % 2)
                    P.op("tensor", lambda e, j=j, pt=pt: e.transpose(
                        pt, yt[:, j * 128:(j + 1) * 128], eye),
                        reads=["yt", "eye"], writes=[rpt])
                    P.op("vector", lambda e, j=j, tt=tt, pt=pt: e.tensor_copy(
                        yT[j][:, tt * 128:(tt + 1) * 128], pt),
                        reads=[rpt], writes=["yT%d" % j])

            psalt = _alt([(psA, "psA"), (psA2, "psA2")])
            for hp in range(8):
                if sink:
                    P.op("tensor", lambda e, hp=hp: e.matmul(
                        psO[:, 0:128], ones[0:1, :], bvrow[:, hp * 128:(hp + 1) * 128],
                        start=True, stop=True),
                        reads=["ones", "bvrow"], writes=["psO"])
                    P.op("vector", lambda e: e.tensor_copy(bvb, psO[:, 0:128]),
                         reads=["psO"], writes=["bvb"])
                    P.dma("sync", lambda e, hp=hp: e.dma_start(
                        bcol[:, 0:1], bq_d[hp * 128:(hp + 1) * 128, :]), writes=["bcol"])
                    P.dma("sync", lambda e, hp=hp: e.dma_start(
                        bcol[:, 1:2], bk_d[hp * 128:(hp + 1) * 128, :]), writes=["bcol"])

                if not rope:
                    for dst, rdst, wsb, wtag, bi in ((QT, "QT", wq, "wq", 0),
                                                     (KT, "KT", wk, "wk", 1)):
                        for nb in range(4):
                            pp, rpp = psalt()
                            for j in range(8):
                                P.op("tensor", lambda e, j=j, nb=nb, pp=pp, wsb=wsb, hp=hp:
                                     e.matmul(pp, wsb[j][:, hp * 128:(hp + 1) * 128],
                                              yT[j][:, nb * 512:(nb + 1) * 512],
                                              start=(j == 0), stop=(j == 7)),
                                     reads=[wtag + str(j), "yT%d" % j], writes=[rpp])
                            if qkvbias:
                                P.op("vector", lambda e, nb=nb, pp=pp, dst=dst, bi=bi:
                                     e.tensor_single_scalar(
                                         dst[:, nb * 512:(nb + 1) * 512], pp,
                                         bcol[:, bi:bi + 1], OPT.add),
                                     reads=[rpp, "bcol"], writes=[rdst])
                            else:
                                P.op("vector", lambda e, nb=nb, pp=pp, dst=dst:
                                     e.tensor_copy(dst[:, nb * 512:(nb + 1) * 512], pp),
                                     reads=[rpp], writes=[rdst])
                else:
                    for dst, rdst, wsb, wtag in ((QT, "QT", wq, "wq"), (KT, "KT", wk, "wk")):
                        for tt in range(16):
                            pp, rpp = psalt()
                            qte = qtm[tt % 2]
                            rqt = "qtm%d" % (tt % 2)
                            for j in range(8):
                                P.op("tensor", lambda e, j=j, tt=tt, pp=pp, wsb=wsb, hp=hp:
                                     e.matmul(pp[:, 0:128],
                                              yT[j][:, tt * 128:(tt + 1) * 128],
                                              wsb[j][:, hp * 128:(hp + 1) * 128],
                                              start=(j == 0), stop=(j == 7)),
                                     reads=[wtag + str(j), "yT%d" % j], writes=[rpp])
                            P.op("scalar", lambda e, pp=pp, qte=qte: e.copy(qte, pp[:, 0:128]),
                                 reads=[rpp], writes=[rqt])
                            for o in (0, 64):
                                P.op("vector", lambda e, qte=qte, o=o, tt=tt: e.tensor_mul(
                                    t32[0], qte[:, o:o + 32], cosb[:, tt * 32:(tt + 1) * 32]),
                                    reads=[rqt, "cosb"], writes=["t320"])
                                P.op("vector", lambda e, qte=qte, o=o, tt=tt: e.tensor_mul(
                                    t32[1], qte[:, o + 32:o + 64], sinb[:, tt * 32:(tt + 1) * 32]),
                                    reads=[rqt, "sinb"], writes=["t321"])
                                P.op("vector", lambda e, o=o: e.tensor_sub(
                                    rof[:, o:o + 32], t32[0], t32[1]),
                                    reads=["t320", "t321"], writes=["rof"])
                                P.op("vector", lambda e, qte=qte, o=o, tt=tt: e.tensor_mul(
                                    t32[0], qte[:, o:o + 32], sinb[:, tt * 32:(tt + 1) * 32]),
                                    reads=[rqt, "sinb"], writes=["t320"])
                                P.op("vector", lambda e, qte=qte, o=o, tt=tt: e.tensor_mul(
                                    t32[1], qte[:, o + 32:o + 64], cosb[:, tt * 32:(tt + 1) * 32]),
                                    reads=[rqt, "cosb"], writes=["t321"])
                                P.op("vector", lambda e, o=o: e.tensor_add(
                                    rof[:, o + 32:o + 64], t32[0], t32[1]),
                                    reads=["t320", "t321"], writes=["rof"])
                            P.op("scalar", lambda e: e.copy(robf, rof),
                                 reads=["rof"], writes=["robf"])
                            pt = psT[tt % 2]
                            rpt = "psT%d" % (tt % 2)
                            P.op("tensor", lambda e, pt=pt: e.transpose(pt, robf, eye),
                                 reads=["robf", "eye"], writes=[rpt])
                            P.op("vector", lambda e, tt=tt, pt=pt, dst=dst: e.tensor_copy(
                                dst[:, tt * 128:(tt + 1) * 128], pt),
                                reads=[rpt], writes=[rdst])

                for tt in range(16):
                    pp, rpp = psalt()
                    for j in range(8):
                        P.op("tensor", lambda e, j=j, tt=tt, pp=pp, hp=hp:
                             e.matmul(pp[:, 0:128], yT[j][:, tt * 128:(tt + 1) * 128],
                                      wv[j][:, hp * 128:(hp + 1) * 128],
                                      start=(j == 0), stop=(j == 7)),
                             reads=["wv" + str(j), "yT%d" % j], writes=[rpp])
                    if sink:
                        P.op("vector", lambda e, tt=tt, pp=pp: e.tensor_add(
                            Vs[:, tt * 128:(tt + 1) * 128], pp[:, 0:128], bvb),
                            reads=[rpp, "bvb"], writes=["Vs"])
                    else:
                        P.op("vector", lambda e, tt=tt, pp=pp: e.tensor_copy(
                            Vs[:, tt * 128:(tt + 1) * 128], pp[:, 0:128]),
                            reads=[rpp], writes=["Vs"])

                for hh in range(2):
                    h = hp * 2 + hh
                    o = hh * 64
                    if t5:
                        for dl in range(16):
                            wn, rwn = win[dl % 2], "win%d" % (dl % 2)
                            off = h * 2304 + 128 * dl
                            P.dma("sync", lambda e, wn=wn, off=off: e.dma_start(
                                wn, AP(tab_h.tensor, off, [[1, 128], [1, 128]])),
                                reads=["tab_h"], writes=[rwn])
                            P.op("tensor", lambda e, wn=wn: e.matmul(
                                psB, wn, jrev, start=True, stop=True),
                                reads=[rwn, "jrev"], writes=["psB"])
                            P.op("scalar", lambda e, dl=dl: e.copy(
                                bias_sb[:, dl * 128:(dl + 1) * 128], psB),
                                reads=["psB"], writes=["bias_sb"])

                    for qt in range(16):
                        W = (qt + 1) * 128 if causal else T
                        nkb = W // 128
                        for nb in range((W + 511) // 512):
                            wcol = min(512, W - nb * 512)
                            pp, rpp = psalt()
                            P.op("tensor", lambda e, pp=pp, qt=qt, nb=nb, wcol=wcol, o=o:
                                 e.matmul(pp[:, 0:wcol],
                                          QT[o:o + 64, qt * 128:(qt + 1) * 128],
                                          KT[o:o + 64, nb * 512:nb * 512 + wcol],
                                          start=True, stop=True),
                                 reads=["QT", "KT"], writes=[rpp])
                            P.op("scalar", lambda e, pp=pp, nb=nb, wcol=wcol: e.mul(
                                S[:, nb * 512:nb * 512 + wcol], pp[:, 0:wcol], 0.125),
                                reads=[rpp], writes=["S"])
                        if t5:
                            for kb in range(qt + 1):
                                dl = qt - kb
                                P.op("vector", lambda e, kb=kb, dl=dl: e.tensor_add(
                                    S[:, kb * 128:(kb + 1) * 128],
                                    S[:, kb * 128:(kb + 1) * 128],
                                    bias_sb[:, dl * 128:(dl + 1) * 128]),
                                    reads=["S", "bias_sb"], writes=["S"])
                        if causal:
                            P.op("vector", lambda e, qt=qt: e.tensor_add(
                                S[:, qt * 128:(qt + 1) * 128],
                                S[:, qt * 128:(qt + 1) * 128], maskt),
                                reads=["S", "maskt"], writes=["S"])
                        P.op("vector", lambda e, W=W: e.reduce_max(stt[:, 0:1], S[:, 0:W], AX),
                             reads=["S"], writes=["stt"])
                        if sink:
                            P.op("vector", lambda e, h=h: e.tensor_max(
                                stt[:, 0:1], stt[:, 0:1], sinkb[:, h:h + 1]),
                                reads=["stt", "sinkb"], writes=["stt"])
                        P.op("vector", lambda e: e.tensor_scalar_mul(stt[:, 1:2], stt[:, 0:1], -1.0),
                             reads=["stt"], writes=["stt"])
                        P.op("scalar", lambda e, W=W: e.activation(
                            Pb[:, 0:W], S[:, 0:W], AF.Exp, bias=stt[:, 1:2], scale=1.0),
                            reads=["S", "stt"], writes=["Pb"])
                        P.op("vector", lambda e, W=W: e.reduce_sum(stt[:, 2:3], Pb[:, 0:W], AX),
                             reads=["Pb"], writes=["stt"])
                        if sink:
                            P.op("scalar", lambda e, h=h: e.activation(
                                stt[:, 4:5], sinkb[:, h:h + 1], AF.Exp, bias=stt[:, 1:2],
                                scale=1.0),
                                reads=["sinkb", "stt"], writes=["stt"])
                            P.op("vector", lambda e: e.tensor_add(
                                stt[:, 2:3], stt[:, 2:3], stt[:, 4:5]),
                                reads=["stt"], writes=["stt"])
                        P.op("vector", lambda e: e.reciprocal(stt[:, 3:4], stt[:, 2:3]),
                             reads=["stt"], writes=["stt"])
                        P.op("vector", lambda e, W=W: e.tensor_single_scalar(
                            Pb[:, 0:W], Pb[:, 0:W], stt[:, 3:4], OPT.mult),
                            reads=["Pb", "stt"], writes=["Pb"])
                        for kb in range(nkb):
                            pt, rpt = psT[kb % 2], "psT%d" % (kb % 2)
                            pts, rpts = PT[kb % 2], "PT%d" % (kb % 2)
                            P.op("tensor", lambda e, kb=kb, pt=pt: e.transpose(
                                pt, Pb[:, kb * 128:(kb + 1) * 128], eye),
                                reads=["Pb", "eye"], writes=[rpt])
                            P.op("scalar", lambda e, pt=pt, pts=pts: e.copy(pts, pt),
                                 reads=[rpt], writes=[rpts])
                            P.op("tensor", lambda e, kb=kb, pts=pts, o=o: e.matmul(
                                psZ, Vs[:, kb * 128 + o:kb * 128 + o + 64], pts,
                                start=(kb == 0), stop=(kb == nkb - 1)),
                                reads=["Vs", rpts], writes=["psZ"])
                        j = h // 2
                        ro = (h % 2) * 64
                        P.op("vector", lambda e, j=j, ro=ro, qt=qt: e.tensor_copy(
                            zT[j][ro:ro + 64, qt * 128:(qt + 1) * 128], psZ),
                            reads=["psZ"], writes=["zT%d" % j])

            for tt in range(16):
                r0 = b * T + tt * 128
                for j in range(8):
                    for c0 in (0, 512):
                        P.op("tensor", lambda e, j=j, tt=tt, c0=c0: e.matmul(
                            psO[:, c0:c0 + 512], zT[j][:, tt * 128:(tt + 1) * 128],
                            wo[j][:, c0:c0 + 512],
                            start=(j == 0), stop=(j == 7)),
                            reads=["zT%d" % j, "wo" + str(j)], writes=["psO"])
                P.dma("sync", lambda e, r0=r0: e.dma_start(xt, x_in[r0:r0 + 128, :]),
                      writes=["xt"])
                P.op("vector", lambda e: e.tensor_add(xm, psO, bob),
                     reads=["psO", "bob"], writes=["xm"])
                xo, rxo = xnew[tt % 2], "xnew%d" % (tt % 2)
                P.op("vector", lambda e, xo=xo: e.tensor_add(xo, xm, xt),
                     reads=["xm", "xt"], writes=[rxo])
                P.dma("gpsimd", lambda e, r0=r0, xo=xo: e.dma_start(
                    x_out[r0:r0 + 128, :], xo),
                    reads=[rxo], writes=["x_out"])

        fin = P.cnt["gpsimd"]
        P.ops["gpsimd"].append(("w", "gpsimd", fin))
        P.finalize(nc)
    return nc


def build_moe():
    nc = bass.Bass("TRN2", target_bir_lowering=False)
    E, FF = 8, 4096
    x_in = nc.dram_tensor("x_in", [NT, D], F32, kind="ExternalInput").ap()
    x_out = nc.dram_tensor("x_out", [NT, D], F32, kind="ExternalOutput").ap()
    wrt_d = nc.dram_tensor("wrt", [D, E], F32, kind="ExternalInput").ap()
    br_d = nc.dram_tensor("br_row", [1, E], F32, kind="ExternalInput").ap()
    w1_d = nc.dram_tensor("w1t", [E, D, FF], F32, kind="ExternalInput").ap()
    b1_d = nc.dram_tensor("b1c", [E, FF, 1], F32, kind="ExternalInput").ap()
    w2_d = nc.dram_tensor("w2t", [E, FF, D], F32, kind="ExternalInput").ap()
    b2_d = nc.dram_tensor("b2_row", [E, 1, D], F32, kind="ExternalInput").ap()
    gam_d = nc.dram_tensor("gam_row", [1, D], F32, kind="ExternalInput").ap()
    bet_d = nc.dram_tensor("bet_row", [1, D], F32, kind="ExternalInput").ap()
    ones_d = nc.dram_tensor("c_ones", [1, 128], F32, kind="ExternalInput").ap()
    eye_d = nc.dram_tensor("c_eye_bf", [128, 128], BF16, kind="ExternalInput").ap()
    ytb_h = nc.dram_tensor("ytb_h", [D, NT], BF16, kind="Internal").ap()
    acc_h = nc.dram_tensor("acc_h", [NT, D], F32, kind="Internal").ap()

    P = Pr()
    with contextlib.ExitStack() as st_:
        _nm = [0]

        def sb(shape, dt):
            _nm[0] += 1
            return st_.enter_context(nc.sbuf_tensor("sb%d" % _nm[0], shape, dt)).ap()

        def ps(shape, dt):
            _nm[0] += 1
            return st_.enter_context(nc.psum_tensor("ps%d" % _nm[0], shape, dt)).ap()

        w1t = [sb([128, FF], BF16) for _ in range(8)]
        w2t = [sb([128, D], BF16) for _ in range(32)]
        TG = 256
        hT = sb([128, 32 * TG], BF16)
        yTg = sb([128, 8 * TG], BF16)
        acc = sb([128, D], F32)
        rw = sb([128, 32 * 8], F32)
        stg = [sb([128, 512], F32) for _ in range(2)]
        xt = sb([128, D], F32)
        xm = sb([128, D], F32)
        yt = sb([128, D], BF16)
        ytb = sb([128, 8 * 128], BF16)
        xnew = [sb([128, D], F32)]
        gamb = sb([128, D], BF16)
        betb = sb([128, D], BF16)
        b2b = sb([128, D], BF16)
        brb = sb([128, E], F32)
        wrtb = [sb([128, E], BF16) for _ in range(8)]
        b1c = sb([128, 32], F32)
        lg = sb([128, E], F32)
        eq1 = sb([128, E], F32)
        lg2 = sb([128, E], F32)
        eq2 = sb([128, E], F32)
        rwt = sb([128, E], F32)
        stt = sb([128, 8], F32)
        ones = sb([1, 128], F32)
        onesb = sb([1, 128], BF16)
        eye = sb([128, 128], BF16)
        grow = sb([1, D], F32)
        growb = sb([1, D], BF16)

        ps1 = [ps([128, 512], F32) for _ in range(2)]
        ps2 = ps([128, 1024], F32)
        tmp2 = ps([128, 1024], F32)
        psL = ps([128, E], F32)
        psT = ps([128, 128], BF16)

        P.dma("sync", lambda e: e.dma_start(ones, ones_d), writes=["ones"])
        P.dma("sync", lambda e: e.dma_start(eye, eye_d), writes=["eye"])
        P.op("scalar", lambda e: e.copy(onesb, ones), reads=["ones"], writes=["onesb"])
        for j in range(8):
            P.dma("sync", lambda e, j=j: e.dma_start(
                stg[0][:, 0:E], wrt_d[j * 128:(j + 1) * 128, :]), writes=["stg0"])
            P.op("scalar", lambda e, j=j: e.copy(wrtb[j], stg[0][:, 0:E]),
                 reads=["stg0"], writes=["wrtb%d" % j])
        P.dma("sync", lambda e: e.dma_start(grow[:, 0:E], br_d), writes=["grow"])
        P.op("tensor", lambda e: e.matmul(psL, ones[0:1, :], grow[:, 0:E],
                                          start=True, stop=True),
             reads=["ones", "grow"], writes=["psL"])
        P.op("vector", lambda e: e.tensor_copy(brb, psL), reads=["psL"], writes=["brb"])
        for src, dst, rdst in ((gam_d, gamb, "gamb"), (bet_d, betb, "betb")):
            P.dma("sync", lambda e, src=src: e.dma_start(grow, src), writes=["grow"])
            P.op("scalar", lambda e: e.copy(growb, grow), reads=["grow"], writes=["growb"])
            for c0 in (0, 512):
                P.op("tensor", lambda e, c0=c0: e.matmul(
                    ps2[:, c0:c0 + 512], onesb[0:1, :], growb[:, c0:c0 + 512],
                    start=True, stop=True),
                    reads=["onesb", "growb"], writes=["ps2"])
            P.op("vector", lambda e, dst=dst: e.tensor_copy(dst, ps2),
                 reads=["ps2"], writes=[rdst])

        for ti in range(32):
            r0 = ti * 128
            P.dma("sync", lambda e, r0=r0: e.dma_start(xt, x_in[r0:r0 + 128, :]),
                  writes=["xt"])
            emit_ln(P, xt, "xt", xm, "xm", stt, "stt")
            P.dma("gpsimd", lambda e, r0=r0: e.dma_start(acc_h[r0:r0 + 128, :], xm),
                  reads=["xm"], writes=["acc_h"])
            P.op("scalar", lambda e: e.copy(yt, xm), reads=["xm"], writes=["yt"])
            for j in range(8):
                P.op("tensor", lambda e, j=j: e.transpose(
                    psT, yt[:, j * 128:(j + 1) * 128], eye),
                    reads=["yt", "eye"], writes=["psT"])
                P.op("vector", lambda e, j=j: e.tensor_copy(
                    ytb[:, j * 128:(j + 1) * 128], psT),
                    reads=["psT"], writes=["ytb"])
                P.dma("gpsimd", lambda e, j=j, ti=ti: e.dma_start(
                    ytb_h[j * 128:(j + 1) * 128, ti * 128:(ti + 1) * 128],
                    ytb[:, j * 128:(j + 1) * 128]),
                    reads=["ytb"], writes=["ytb_h"])
            for j in range(8):
                P.op("tensor", lambda e, j=j: e.matmul(
                    psL, ytb[:, j * 128:(j + 1) * 128], wrtb[j],
                    start=(j == 0), stop=(j == 7)),
                    reads=["ytb", "wrtb%d" % j], writes=["psL"])
            P.op("vector", lambda e: e.tensor_add(lg, psL, brb),
                 reads=["psL", "brb"], writes=["lg"])
            P.op("vector", lambda e: e.reduce_max(stt[:, 0:1], lg, AX),
                 reads=["lg"], writes=["stt"])
            P.op("vector", lambda e: e.tensor_single_scalar(
                eq1, lg, stt[:, 0:1], OPT.is_equal), reads=["lg", "stt"], writes=["eq1"])
            P.op("vector", lambda e: e.scalar_tensor_tensor(
                lg2, eq1, NEG, lg, OPT.mult, OPT.add),
                reads=["eq1", "lg"], writes=["lg2"])
            P.op("vector", lambda e: e.reduce_max(stt[:, 1:2], lg2, AX),
                 reads=["lg2"], writes=["stt"])
            P.op("vector", lambda e: e.tensor_single_scalar(
                eq2, lg2, stt[:, 1:2], OPT.is_equal), reads=["lg2", "stt"], writes=["eq2"])
            P.op("vector", lambda e: e.tensor_sub(stt[:, 2:3], stt[:, 1:2], stt[:, 0:1]),
                 reads=["stt"], writes=["stt"])
            P.op("scalar", lambda e: e.activation(stt[:, 3:4], stt[:, 2:3], AF.Exp),
                 reads=["stt"], writes=["stt"])
            P.op("vector", lambda e: e.tensor_scalar_add(stt[:, 4:5], stt[:, 3:4], 1.0),
                 reads=["stt"], writes=["stt"])
            P.op("vector", lambda e: e.reciprocal(stt[:, 5:6], stt[:, 4:5]),
                 reads=["stt"], writes=["stt"])
            P.op("vector", lambda e: e.tensor_scalar(
                stt[:, 6:7], stt[:, 5:6], -1.0, 1.0, OPT.mult, OPT.add),
                reads=["stt"], writes=["stt"])
            P.op("vector", lambda e: e.tensor_single_scalar(
                rwt, eq1, stt[:, 5:6], OPT.mult), reads=["eq1", "stt"], writes=["rwt"])
            P.op("vector", lambda e, ti=ti: e.scalar_tensor_tensor(
                rw[:, ti * 8:(ti + 1) * 8], eq2, stt[:, 6:7], rwt, OPT.mult, OPT.add),
                reads=["eq2", "stt", "rwt"], writes=["rw"])

        for ee in range(E):
            emit_load_bf16(P, w1_d[ee], w1t, stg, ["stg0", "stg1"], "w1t", FF, chunk=512)
            emit_load_bf16(P, w2_d[ee], w2t, stg, ["stg0", "stg1"], "w2t", D, chunk=512)
            for ft in range(32):
                P.dma("sync", lambda e, ee=ee, ft=ft: e.dma_start(
                    b1c[:, ft:ft + 1], b1_d[ee, ft * 128:(ft + 1) * 128, :]),
                    writes=["b1c"])
            P.dma("sync", lambda e, ee=ee: e.dma_start(grow, b2_d[ee]), writes=["grow"])
            P.op("scalar", lambda e: e.copy(growb, grow), reads=["grow"], writes=["growb"])
            for c0 in (0, 512):
                P.op("tensor", lambda e, c0=c0: e.matmul(
                    tmp2[:, c0:c0 + 512], onesb[0:1, :], growb[:, c0:c0 + 512],
                    start=True, stop=True),
                    reads=["onesb", "growb"], writes=["tmp2"])
            P.op("vector", lambda e: e.tensor_copy(b2b, tmp2),
                 reads=["tmp2"], writes=["b2b"])

            for g in range(16):
                TG = 256
                for j in range(8):
                    P.dma("sync", lambda e, j=j, g=g, TG=TG: e.dma_start(
                        yTg[:, j * TG:(j + 1) * TG],
                        ytb_h[j * 128:(j + 1) * 128, g * TG:(g + 1) * TG]),
                        reads=["ytb_h"], writes=["yTg"])
                for ft in range(32):
                    pp, rpp = ps1[ft % 2], "ps1%d" % (ft % 2)
                    for j in range(8):
                        P.op("tensor", lambda e, j=j, ft=ft, pp=pp, TG=TG: e.matmul(
                            pp[:, 0:TG], w1t[j][:, ft * 128:(ft + 1) * 128],
                            yTg[:, j * TG:(j + 1) * TG],
                            start=(j == 0), stop=(j == 7)),
                            reads=["w1t%d" % j, "yTg"], writes=[rpp])
                    P.op("scalar", lambda e, ft=ft, pp=pp, TG=TG: e.activation(
                        hT[:, ft * TG:(ft + 1) * TG], pp[:, 0:TG], AF.Relu,
                        bias=b1c[:, ft:ft + 1], scale=1.0),
                        reads=[rpp, "b1c"], writes=["hT"])
                for tt in range(2):
                    ti = g * 2 + tt
                    r0 = ti * 128
                    for ft in range(32):
                        for c0 in (0, 512):
                            P.op("tensor", lambda e, ft=ft, tt=tt, c0=c0, TG=TG: e.matmul(
                                ps2[:, c0:c0 + 512],
                                hT[:, ft * TG + tt * 128:ft * TG + (tt + 1) * 128],
                                w2t[ft][:, c0:c0 + 512],
                                start=(ft == 0), stop=(ft == 31)),
                                reads=["hT", "w2t%d" % ft], writes=["ps2"])
                    P.dma("sync", lambda e, r0=r0: e.dma_start(acc, acc_h[r0:r0 + 128, :]),
                          reads=["acc_h"], writes=["acc"])
                    P.op("vector", lambda e: e.tensor_add(tmp2, ps2, b2b),
                         reads=["ps2", "b2b"], writes=["tmp2"])
                    P.op("vector", lambda e, ti=ti, ee=ee: e.scalar_tensor_tensor(
                        acc, tmp2, rw[:, ti * 8 + ee:ti * 8 + ee + 1], acc,
                        OPT.mult, OPT.add),
                        reads=["tmp2", "rw", "acc"], writes=["acc"])
                    P.dma("gpsimd", lambda e, r0=r0: e.dma_start(
                        acc_h[r0:r0 + 128, :], acc),
                        reads=["acc"], writes=["acc_h"])

        for ti in range(32):
            r0 = ti * 128
            P.dma("sync", lambda e, r0=r0: e.dma_start(xt, acc_h[r0:r0 + 128, :]),
                  reads=["acc_h"], writes=["xt"])
            emit_ln(P, xt, "xt", xm, "xm", stt, "stt", gb=gamb, bb=betb,
                    rgb="gamb", rbb="betb")
            P.dma("sync", lambda e, r0=r0: e.dma_start(acc, x_in[r0:r0 + 128, :]),
                  writes=["acc"])
            xo, rxo = xnew[0], "xnew0"
            P.op("vector", lambda e, xo=xo: e.tensor_add(xo, xm, acc),
                 reads=["xm", "acc"], writes=[rxo])
            P.dma("gpsimd", lambda e, r0=r0, xo=xo: e.dma_start(
                x_out[r0:r0 + 128, :], xo),
                reads=[rxo], writes=["x_out"])

        fin = P.cnt["gpsimd"]
        P.ops["gpsimd"].append(("w", "gpsimd", fin))
        P.finalize(nc)
    return nc


def build_rwkv():
    nc = bass.Bass("TRN2", target_bir_lowering=False)
    x_in = nc.dram_tensor("x_in", [NT, D], F32, kind="ExternalInput").ap()
    x_out = nc.dram_tensor("x_out", [NT, D], F32, kind="ExternalOutput").ap()
    wr_d = nc.dram_tensor("wrt", [D, D], F32, kind="ExternalInput").ap()
    wk_d = nc.dram_tensor("wkt", [D, D], F32, kind="ExternalInput").ap()
    wv_d = nc.dram_tensor("wvt", [D, D], F32, kind="ExternalInput").ap()
    mur_d = nc.dram_tensor("mur_col", [D, 1], F32, kind="ExternalInput").ap()
    muk_d = nc.dram_tensor("muk_col", [D, 1], F32, kind="ExternalInput").ap()
    eye_d = nc.dram_tensor("c_eye_bf", [128, 128], BF16, kind="ExternalInput").ap()

    P = Pr()
    with contextlib.ExitStack() as st_:
        _nm = [0]

        def sb(shape, dt):
            _nm[0] += 1
            return st_.enter_context(nc.sbuf_tensor("sb%d" % _nm[0], shape, dt)).ap()

        def ps(shape, dt):
            _nm[0] += 1
            return st_.enter_context(nc.psum_tensor("ps%d" % _nm[0], shape, dt)).ap()

        buf = [sb([128, T + 1], BF16) for _ in range(8)]
        xrT = [sb([128, T], BF16) for _ in range(8)]
        xkT = [sb([128, T], BF16) for _ in range(8)]
        wrt = [sb([128, D], BF16) for _ in range(8)]
        wkt = [sb([128, D], BF16) for _ in range(8)]
        wvt = [sb([128, D], BF16) for _ in range(8)]
        dbuf = sb([128, T], F32)
        k2T = sb([128, D], BF16)
        rsb = sb([128, D], F32)
        krel = sb([128, D], F32)
        k2b = sb([128, D], BF16)
        stg = [sb([128, 1024], F32) for _ in range(2)]
        xt = sb([128, D], F32)
        xm = sb([128, D], F32)
        yt = sb([128, D], BF16)
        xnew = [sb([128, D], F32) for _ in range(2)]
        muc = sb([128, 2], F32)
        stt = sb([128, 8], F32)
        eye = sb([128, 128], BF16)

        psR = ps([128, 1024], F32)
        psK = ps([128, 1024], F32)
        psO = ps([128, 1024], F32)
        psT = [ps([128, 128], BF16) for _ in range(2)]

        P.dma("sync", lambda e: e.dma_start(eye, eye_d), writes=["eye"])
        emit_load_bf16(P, wr_d, wrt, stg, ["stg0", "stg1"], "wrt", D)
        emit_load_bf16(P, wk_d, wkt, stg, ["stg0", "stg1"], "wkt", D)
        emit_load_bf16(P, wv_d, wvt, stg, ["stg0", "stg1"], "wvt", D)

        for b in range(2):
            for j in range(8):
                P.op("vector", lambda e, j=j: e.memset(buf[j][:, 0:1], 0.0),
                     reads=[], writes=["buf%d" % j])
            for tt in range(16):
                r0 = b * T + tt * 128
                P.dma("sync", lambda e, r0=r0: e.dma_start(xt, x_in[r0:r0 + 128, :]),
                      writes=["xt"])
                emit_ln(P, xt, "xt", xm, "xm", stt, "stt")
                P.op("scalar", lambda e: e.copy(yt, xm), reads=["xm"], writes=["yt"])
                for j in range(8):
                    pt, rpt = psT[j % 2], "psT%d" % (j % 2)
                    P.op("tensor", lambda e, j=j, pt=pt: e.transpose(
                        pt, yt[:, j * 128:(j + 1) * 128], eye),
                        reads=["yt", "eye"], writes=[rpt])
                    P.op("vector", lambda e, j=j, tt=tt, pt=pt: e.tensor_copy(
                        buf[j][:, 1 + tt * 128:1 + (tt + 1) * 128], pt),
                        reads=[rpt], writes=["buf%d" % j])
            for j in range(8):
                P.dma("sync", lambda e, j=j: e.dma_start(
                    muc[:, 0:1], mur_d[j * 128:(j + 1) * 128, :]), writes=["muc"])
                P.dma("sync", lambda e, j=j: e.dma_start(
                    muc[:, 1:2], muk_d[j * 128:(j + 1) * 128, :]), writes=["muc"])
                P.op("vector", lambda e, j=j: e.tensor_sub(
                    dbuf, buf[j][:, 1:T + 1], buf[j][:, 0:T]),
                    reads=["buf%d" % j], writes=["dbuf"])
                P.op("vector", lambda e, j=j: e.scalar_tensor_tensor(
                    xrT[j], dbuf, muc[:, 0:1], buf[j][:, 0:T], OPT.mult, OPT.add),
                    reads=["dbuf", "muc", "buf%d" % j], writes=["xrT%d" % j])
                P.op("vector", lambda e, j=j: e.scalar_tensor_tensor(
                    xkT[j], dbuf, muc[:, 1:2], buf[j][:, 0:T], OPT.mult, OPT.add),
                    reads=["dbuf", "muc", "buf%d" % j], writes=["xkT%d" % j])
            for tt in range(16):
                r0 = b * T + tt * 128
                for j in range(8):
                    for c0 in (0, 512):
                        P.op("tensor", lambda e, j=j, tt=tt, c0=c0: e.matmul(
                            psR[:, c0:c0 + 512], xrT[j][:, tt * 128:(tt + 1) * 128],
                            wrt[j][:, c0:c0 + 512],
                            start=(j == 0), stop=(j == 7)),
                            reads=["xrT%d" % j, "wrt%d" % j], writes=["psR"])
                P.op("scalar", lambda e: e.activation(rsb, psR, AF.Sigmoid),
                     reads=["psR"], writes=["rsb"])
                for j in range(8):
                    for c0 in (0, 512):
                        P.op("tensor", lambda e, j=j, tt=tt, c0=c0: e.matmul(
                            psK[:, c0:c0 + 512], xkT[j][:, tt * 128:(tt + 1) * 128],
                            wkt[j][:, c0:c0 + 512],
                            start=(j == 0), stop=(j == 7)),
                            reads=["xkT%d" % j, "wkt%d" % j], writes=["psK"])
                P.op("scalar", lambda e: e.activation(krel, psK, AF.Relu),
                     reads=["psK"], writes=["krel"])
                P.op("scalar", lambda e: e.square(k2b, krel),
                     reads=["krel"], writes=["k2b"])
                for j in range(8):
                    pt, rpt = psT[j % 2], "psT%d" % (j % 2)
                    P.op("tensor", lambda e, j=j, pt=pt: e.transpose(
                        pt, k2b[:, j * 128:(j + 1) * 128], eye),
                        reads=["k2b", "eye"], writes=[rpt])
                    P.op("vector", lambda e, j=j, pt=pt: e.tensor_copy(
                        k2T[:, j * 128:(j + 1) * 128], pt),
                        reads=[rpt], writes=["k2T"])
                for j in range(8):
                    for c0 in (0, 512):
                        P.op("tensor", lambda e, j=j, c0=c0: e.matmul(
                            psO[:, c0:c0 + 512], k2T[:, j * 128:(j + 1) * 128],
                            wvt[j][:, c0:c0 + 512],
                            start=(j == 0), stop=(j == 7)),
                            reads=["k2T", "wvt%d" % j], writes=["psO"])
                P.dma("sync", lambda e, r0=r0: e.dma_start(xt, x_in[r0:r0 + 128, :]),
                      writes=["xt"])
                P.op("vector", lambda e: e.tensor_mul(xm, psO, rsb),
                     reads=["psO", "rsb"], writes=["xm"])
                xo, rxo = xnew[tt % 2], "xnew%d" % (tt % 2)
                P.op("vector", lambda e, xo=xo: e.tensor_add(xo, xm, xt),
                     reads=["xm", "xt"], writes=[rxo])
                P.dma("gpsimd", lambda e, r0=r0, xo=xo: e.dma_start(
                    x_out[r0:r0 + 128, :], xo),
                    reads=[rxo], writes=["x_out"])

        fin = P.cnt["gpsimd"]
        P.ops["gpsimd"].append(("w", "gpsimd", fin))
        P.finalize(nc)
    return nc


_NC_CACHE = {}


def _get_nc(kind):
    if kind not in _NC_CACHE:
        if kind == "moe":
            _NC_CACHE[kind] = build_moe()
        elif kind == "rwkv":
            _NC_CACHE[kind] = build_rwkv()
        else:
            _NC_CACHE[kind] = build_attn(kind)
    return _NC_CACHE[kind]


def _consts():
    c = {}
    c["c_ones"] = np.ones((1, 128), np.float32)
    c["c_eye_bf"] = np.eye(128).astype(BF)
    c["c_eye_f"] = np.eye(128).astype(np.float32)
    c["c_mask"] = np.triu(np.full((128, 128), NEG, np.float32), k=1)
    half = 32
    theta = 1.0 / 10000.0 ** (np.arange(half, dtype=np.float32) / half)
    ang = np.arange(T, dtype=np.float32)[:, None] * theta[None, :]
    c["cos_tm"] = np.cos(ang).astype(np.float32)
    c["sin_tm"] = np.sin(ang).astype(np.float32)
    oh = np.zeros((32, 2304), np.float32)
    for u in range(2304):
        n = max(u - 127, 0)
        oh[_bucket(n), u] = 1.0
    c["oh_bf"] = oh.astype(BF)
    c["j_bf"] = np.flipud(np.eye(128)).astype(BF)
    return c


def kernel(**inputs):
    ii = {k: np.asarray(v) for k, v in inputs.items()}
    C = _consts()
    ca = np.ascontiguousarray

    x = ii["L0_attn_a_x"].reshape(8, 2, T, D)
    shards = [ca(x[i]).reshape(NT, D) for i in range(8)]

    wqkv0 = ii["L0_attn_a_W_QKV"]
    m_l0a = {
        "wq": ca(wqkv0[:, 0:D]), "wk": ca(wqkv0[:, D:2 * D]), "wv": ca(wqkv0[:, 2 * D:3 * D]),
        "wo": ca(ii["L0_attn_a_W_O"]),
        "bo_row": ca(ii["L0_attn_a_b_O"][None, :]),
        "bq_col": ca(ii["L0_attn_a_b_QKV"][0:D, None]),
        "bk_col": ca(ii["L0_attn_a_b_QKV"][D:2 * D, None]),
        "bv_row": ca(ii["L0_attn_a_b_QKV"][None, 2 * D:3 * D]),
        "sink_row": ca(ii["L0_attn_a_sink_logit"][None, :]),
        "c_ones": C["c_ones"], "c_eye_bf": C["c_eye_bf"], "c_mask": C["c_mask"],
    }
    wb = ii["L0_attn_b_w_qkv"]
    m_l0b = {
        "wq": ca(wb[:, 0:64, :].transpose(2, 0, 1)).reshape(D, D),
        "wk": ca(wb[:, 64:128, :].transpose(2, 0, 1)).reshape(D, D),
        "wv": ca(wb[:, 128:192, :].transpose(2, 0, 1)).reshape(D, D),
        "wo": ca(ii["L0_attn_b_w_proj"].T),
        "bo_row": ca(ii["L0_attn_b_b_proj"][None, :]),
        "c_ones": C["c_ones"], "c_eye_bf": C["c_eye_bf"], "c_mask": C["c_mask"],
    }
    m_moe = {
        "wrt": ca(ii["L0_ffn_Wr"].T),
        "br_row": ca(ii["L0_ffn_br"][None, :]),
        "w1t": ca(ii["L0_ffn_W1"].transpose(0, 2, 1)),
        "b1c": ca(ii["L0_ffn_b1"][:, :, None]),
        "w2t": ca(ii["L0_ffn_W2"].transpose(0, 2, 1)),
        "b2_row": ca(ii["L0_ffn_b2"][:, None, :]),
        "gam_row": ca(ii["L0_ffn_gamma"][None, :]),
        "bet_row": ca(ii["L0_ffn_beta"][None, :]),
        "c_ones": C["c_ones"], "c_eye_bf": C["c_eye_bf"],
    }
    wa1 = ii["L1_attn_a_w_qkv"]
    m_l1a = {
        "wq": ca(wa1[:, 0:D]), "wk": ca(wa1[:, D:2 * D]), "wv": ca(wa1[:, 2 * D:3 * D]),
        "wo": ca(ii["L1_attn_a_w_out"]),
        "bo_row": ca(ii["L1_attn_a_b_out"][None, :]),
        "cos_tm": C["cos_tm"], "sin_tm": C["sin_tm"],
        "c_ones": C["c_ones"], "c_eye_bf": C["c_eye_bf"],
    }
    m_l1b = {
        "wq": ca(ii["L1_attn_b_Wq"]), "wk": ca(ii["L1_attn_b_Wk"]),
        "wv": ca(ii["L1_attn_b_Wv"]), "wo": ca(ii["L1_attn_b_Wo"]),
        "bo_row": ca(ii["L1_attn_b_bo"][None, :]),
        "relemb": ca(ii["L1_attn_b_rel_emb"]),
        "oh_bf": C["oh_bf"], "j_bf": C["j_bf"],
        "c_ones": C["c_ones"], "c_eye_bf": C["c_eye_bf"], "c_mask": C["c_mask"],
        "c_eye_f": C["c_eye_f"],
    }
    m_rwkv = {
        "wrt": ca(ii["L1_ffn_W_r"].T), "wkt": ca(ii["L1_ffn_W_k"].T),
        "wvt": ca(ii["L1_ffn_W_v"].T),
        "mur_col": ca(ii["L1_ffn_mu_r"][:, None]),
        "muk_col": ca(ii["L1_ffn_mu_k"][:, None]),
        "c_eye_bf": C["c_eye_bf"],
    }

    plan = [("l0a", m_l0a), ("moe", m_moe), ("l0b", m_l0b), ("moe", m_moe),
            ("l1a", m_l1a), ("rwkv", m_rwkv), ("l1b", m_l1b), ("rwkv", m_rwkv)]
    cur = shards
    for kind, wm in plan:
        nc = _get_nc(kind)
        in_maps = [dict(wm, x_in=cur[i]) for i in range(8)]
        res = run_bass_kernel_spmd(nc, in_maps, core_ids=list(range(8)))
        cur = [np.asarray(r["x_out"]) for r in res.results]
    return np.concatenate([c.reshape(2, T, D) for c in cur], axis=0)


# version 03 (final)
# speedup vs baseline: 1.0005x; 1.0005x over previous
import contextlib
import math

import numpy as np
import ml_dtypes

import concourse.bass as bass
import concourse.mybir as mybir
from concourse.bass import AP
from concourse.bass_utils import run_bass_kernel_spmd

F32 = mybir.dt.float32
BF16 = mybir.dt.bfloat16
AX = mybir.AxisListType.X
AF = mybir.ActivationFunctionType
OPT = mybir.AluOpType
BF = ml_dtypes.bfloat16

NT = 4096
T = 2048
D = 1024
NH = 16
NEG = -1.0e30

ENGS = ("tensor", "vector", "scalar", "sync", "gpsimd")


class Pr:

    def __init__(self):
        self.phase = -1
        self.ops = None
        self.cnt = {}
        self.lastw = {}
        self.lastr = {}
        self.seen = {e: {} for e in ENGS}
        self.sems = {}

    def new_phase(self):
        snap = []
        if self.phase >= 0:
            snap = [((self.phase, e), self.cnt.get((self.phase, e), 0)) for e in ENGS]
        self.phase += 1
        self.ops = {e: [] for e in ENGS}
        for e in ENGS:
            for key, c in snap:
                self._wait(e, key, c)

    def _wait(self, eng, key, pc):
        if pc <= 0:
            return
        if key[1] == eng and eng in ("tensor", "sync", "gpsimd"):
            return
        if self.seen[eng].get(key, 0) >= pc:
            return
        self.seen[eng][key] = pc
        self.ops[eng].append(("w", key, pc))

    def op(self, eng, fn, reads=(), writes=(), inc=1):
        pf = "%d:" % self.phase
        reads = [pf + r for r in reads]
        writes = [pf + w for w in writes]
        deps = []
        for r in reads:
            if r in self.lastw:
                deps.append(self.lastw[r])
        for w in writes:
            if w in self.lastw:
                deps.append(self.lastw[w])
            for key, pc in self.lastr.get(w, {}).items():
                deps.append((key, pc))
        for key, pc in deps:
            self._wait(eng, key, pc)
        self.ops[eng].append(("i", fn, inc))
        ck = (self.phase, eng)
        self.cnt[ck] = self.cnt.get(ck, 0) + inc
        c = self.cnt[ck]
        for r in reads:
            self.lastr.setdefault(r, {})[ck] = c
        for w in writes:
            self.lastw[w] = (ck, c)
            self.lastr[w] = {}

    def dma(self, eng, fn, reads=(), writes=()):
        self.op(eng, fn, reads, writes, inc=16)

    def final_wait(self):
        ck = (self.phase, "gpsimd")
        self.ops["gpsimd"].append(("w", ck, self.cnt.get(ck, 0)))

    def finalize_phase(self, nc, stack):
        ph = self.phase
        for e in ENGS:
            self.sems[(ph, e)] = stack.enter_context(
                nc.semaphore("sem_p%d_%s" % (ph, e)))
        sems = self.sems
        with nc.Block("blk%d" % ph) as blk:
            for e in ENGS:
                items = self.ops[e]

                def body(eng, items=items, e=e, ph=ph):
                    for it in items:
                        if it[0] == "w":
                            eng.wait_ge(sems[it[1]], it[2])
                        else:
                            it[1](eng).then_inc(sems[(ph, e)], it[2])

                getattr(blk, e)(body)


def _alt(buflist):
    i = [0]

    def nxt():
        b = buflist[i[0] % len(buflist)]
        i[0] += 1
        return b
    return nxt


def emit_ln(P, xt, rxt, xm, rxm, st, rst, gb=None, bb=None, rgb=None, rbb=None):
    out, rout = xm, rxm
    P.op("vector", lambda e, xt=xt, st=st: e.reduce_sum(st[:, 0:1], xt, AX),
         reads=[rxt], writes=[rst])
    P.op("vector", lambda e, st=st: e.tensor_scalar_mul(st[:, 1:2], st[:, 0:1], 1.0 / D),
         reads=[rst], writes=[rst])
    P.op("vector", lambda e, xt=xt, xm=xm, st=st:
         e.tensor_single_scalar(xm, xt, st[:, 1:2], OPT.subtract),
         reads=[rxt, rst], writes=[rxm])
    P.op("scalar", lambda e, xt=xt, xm=xm: e.square(xt, xm),
         reads=[rxm], writes=[rxt])
    P.op("vector", lambda e, xt=xt, st=st: e.reduce_sum(st[:, 2:3], xt, AX),
         reads=[rxt], writes=[rst])
    P.op("vector", lambda e, st=st:
         e.tensor_scalar(st[:, 3:4], st[:, 2:3], 1.0 / D, 1e-5, OPT.mult, OPT.add),
         reads=[rst], writes=[rst])
    P.op("scalar", lambda e, st=st: e.activation(st[:, 4:5], st[:, 3:4], AF.Sqrt),
         reads=[rst], writes=[rst])
    P.op("vector", lambda e, st=st: e.reciprocal(st[:, 5:6], st[:, 4:5]),
         reads=[rst], writes=[rst])
    P.op("vector", lambda e, xm=xm, st=st, out=out:
         e.tensor_single_scalar(out, xm, st[:, 5:6], OPT.mult),
         reads=[rxm, rst], writes=[rout])
    if gb is not None:
        P.op("vector", lambda e, out=out, gb=gb: e.tensor_mul(out, out, gb),
             reads=[rout, rgb], writes=[rout])
        P.op("vector", lambda e, out=out, bb=bb: e.tensor_add(out, out, bb),
             reads=[rout, rbb], writes=[rout])


def emit_load_bf16(P, dram, dsts, stages, rstages, tag, n_free, chunk=1024):
    nxt = _alt(list(range(len(stages))))
    for j, dst in enumerate(dsts):
        for c0 in range(0, n_free, chunk):
            si = nxt()
            stg, rstg = stages[si], rstages[si]
            w = min(chunk, n_free - c0)
            P.dma("sync", lambda e, stg=stg, w=w, j=j, c0=c0:
                  e.dma_start(stg[:, 0:w], dram[j * 128:(j + 1) * 128, c0:c0 + w]),
                  reads=[], writes=[rstg])
            P.op("scalar", lambda e, stg=stg, w=w, dst=dst, c0=c0:
                 e.copy(dst[:, c0:c0 + w], stg[:, 0:w]),
                 reads=[rstg], writes=[tag + str(j)])


def emit_bcast_row(P, psum, rpsum, ones, rones, row_ap, rrow, dst, rdst, n):
    for c0 in range(0, n, 512):
        w = min(512, n - c0)
        P.op("tensor", lambda e, c0=c0, w=w:
             e.matmul(psum[:, c0:c0 + w], ones, row_ap[:, c0:c0 + w],
                      start=True, stop=True),
             reads=[rones, rrow], writes=[rpsum])
    P.op("vector", lambda e: e.tensor_copy(dst, psum[:, 0:n]),
         reads=[rpsum], writes=[rdst])


def _bucket(n):
    if n < 16:
        return n
    v = 16 + int(math.log(n / 16.0) / math.log(8.0) * 16.0)
    return min(v, 31)


def emit_attn(nc, P, d, sb, ps, flavor):
    causal = flavor in ("l0a", "l0b", "l1b")
    sink = flavor == "l0a"
    qkvbias = flavor == "l0a"
    rope = flavor == "l1a"
    t5 = flavor == "l1b"

    x_in = d["x_in"]
    x_out = d["x_out"]
    wq_d, wk_d, wv_d, wo_d, bo_d = d["wq"], d["wk"], d["wv"], d["wo"], d["bo_row"]
    ones_d, eye_d = d["c_ones"], d["c_eye_bf"]
    if causal:
        mask_d = d["c_mask"]
    if qkvbias:
        bq_d, bk_d, bv_d, sink_d = d["bq_col"], d["bk_col"], d["bv_row"], d["sink_row"]
    if rope:
        cos_d, sin_d = d["cos_tm"], d["sin_tm"]
    if t5:
        oh_d, jrev_d, rel_d, tab_h = d["oh_bf"], d["j_bf"], d["relemb"], d["tab_h"]

    yT = [sb([128, T], BF16) for _ in range(8)]
    zT = [sb([128, T], BF16) for _ in range(8)]
    wq = [sb([128, D], BF16) for _ in range(8)]
    wk = [sb([128, D], BF16) for _ in range(8)]
    wv = [sb([128, D], BF16) for _ in range(8)]
    wo = [sb([128, D], BF16) for _ in range(8)]
    QT = sb([128, T], BF16)
    KT = sb([128, T], BF16)
    Vs = sb([128, T], BF16)
    S = sb([128, T], F32)
    Pb = sb([128, T], BF16)
    PT = [sb([128, 128], BF16) for _ in range(2)]
    stg = [sb([128, 1024], F32) for _ in range(2)]
    xt = sb([128, D], F32)
    xm = sb([128, D], F32)
    yt = sb([128, D], BF16)
    xnew = [sb([128, D], F32) for _ in range(2)]
    bob = sb([128, D], F32)
    stt = sb([128, 8], F32)
    ones = sb([1, 128], F32)
    eye = sb([128, 128], BF16)
    if causal:
        maskt = sb([128, 128], F32)
    if sink:
        sinkb = sb([128, NH], F32)
        bvb = sb([128, 128], F32)
        bvrow = sb([1, D], F32)
        bcol = sb([128, 2], F32)
        sinkrow = sb([1, NH], F32)
    if rope:
        cosb = sb([128, 16 * 32], F32)
        sinb = sb([128, 16 * 32], F32)
        qtm = [sb([128, 128], F32) for _ in range(2)]
        rof = sb([128, 128], F32)
        robf = sb([128, 128], BF16)
        t32 = [sb([128, 32], F32) for _ in range(2)]
    if t5:
        ohb = sb([32, 2304], BF16)
        relf = sb([32, NH], F32)
        relb = sb([32, NH], BF16)
        tabb = sb([128, NH], BF16)
        tabT = sb([16, 2304], BF16)
        jrev = sb([128, 128], BF16)
        win = [sb([128, 128], BF16) for _ in range(2)]
        bias_sb = sb([128, 16 * 128], F32)
    borow = sb([1, D], F32)

    psA = ps([128, 512], F32)
    psA2 = ps([128, 512], F32)
    psT = [ps([128, 128], BF16) for _ in range(2)]
    psZ = ps([64, 128], F32)
    psO = ps([128, 1024], F32)
    if t5:
        psB = ps([128, 128], F32)
        psR = psB[:, 0:16]

    P.dma("sync", lambda e: e.dma_start(ones, ones_d), writes=["ones"])
    P.dma("sync", lambda e: e.dma_start(eye, eye_d), writes=["eye"])
    P.dma("sync", lambda e: e.dma_start(borow, bo_d), writes=["borow"])
    if causal:
        P.dma("sync", lambda e: e.dma_start(maskt, mask_d), writes=["maskt"])
    emit_load_bf16(P, wq_d, wq, stg, ["stg0", "stg1"], "wq", D)
    emit_load_bf16(P, wk_d, wk, stg, ["stg0", "stg1"], "wk", D)
    emit_load_bf16(P, wv_d, wv, stg, ["stg0", "stg1"], "wv", D)
    emit_load_bf16(P, wo_d, wo, stg, ["stg0", "stg1"], "wo", D)
    emit_bcast_row(P, psO, "psO", ones[0:1, :], "ones", borow, "borow",
                   bob, "bob", D)
    if sink:
        P.dma("sync", lambda e: e.dma_start(bvrow, bv_d), writes=["bvrow"])
        P.dma("sync", lambda e: e.dma_start(sinkrow, sink_d), writes=["sinkrow"])
        P.op("tensor", lambda e: e.matmul(psO[:, 0:NH], ones[0:1, :],
                                          sinkrow, start=True, stop=True),
             reads=["ones", "sinkrow"], writes=["psO"])
        P.op("vector", lambda e: e.tensor_copy(sinkb, psO[:, 0:NH]),
             reads=["psO"], writes=["sinkb"])
    if rope:
        for tt in range(16):
            P.dma("sync", lambda e, tt=tt: e.dma_start(
                cosb[:, tt * 32:(tt + 1) * 32], cos_d[tt * 128:(tt + 1) * 128, :]),
                writes=["cosb"])
            P.dma("sync", lambda e, tt=tt: e.dma_start(
                sinb[:, tt * 32:(tt + 1) * 32], sin_d[tt * 128:(tt + 1) * 128, :]),
                writes=["sinb"])
    if t5:
        P.dma("sync", lambda e: e.dma_start(ohb, oh_d), writes=["ohb"])
        P.dma("sync", lambda e: e.dma_start(jrev, jrev_d), writes=["jrev"])
        P.dma("sync", lambda e: e.dma_start(relf, rel_d), writes=["relf"])
        P.op("scalar", lambda e: e.copy(relb, relf), reads=["relf"], writes=["relb"])
        for ut in range(18):
            P.op("tensor", lambda e, ut=ut: e.matmul(
                psR, ohb[:, ut * 128:(ut + 1) * 128], relb, start=True, stop=True),
                reads=["ohb", "relb"], writes=["psB"])
            P.op("scalar", lambda e: e.copy(tabb, psR),
                 reads=["psB"], writes=["tabb"])
            P.op("tensor", lambda e: e.transpose(psT[0][0:16, :], tabb, eye),
                 reads=["tabb", "eye"], writes=["psT0"])
            P.op("vector", lambda e, ut=ut: e.tensor_copy(
                tabT[:, ut * 128:(ut + 1) * 128], psT[0][0:16, :]),
                reads=["psT0"], writes=["tabT"])
        P.dma("gpsimd", lambda e: e.dma_start(tab_h, tabT),
              reads=["tabT"], writes=["tab_hr"])

    for b in range(2):
        for tt in range(16):
            r0 = b * T + tt * 128
            P.dma("sync", lambda e, r0=r0: e.dma_start(xt, x_in[r0:r0 + 128, :]),
                  writes=["xt"])
            emit_ln(P, xt, "xt", xm, "xm", stt, "stt")
            P.op("scalar", lambda e: e.copy(yt, xm), reads=["xm"], writes=["yt"])
            for j in range(8):
                pt = psT[j % 2]
                rpt = "psT%d" % (j % 2)
                P.op("tensor", lambda e, j=j, pt=pt: e.transpose(
                    pt, yt[:, j * 128:(j + 1) * 128], eye),
                    reads=["yt", "eye"], writes=[rpt])
                P.op("vector", lambda e, j=j, tt=tt, pt=pt: e.tensor_copy(
                    yT[j][:, tt * 128:(tt + 1) * 128], pt),
                    reads=[rpt], writes=["yT%d" % j])

        psalt = _alt([(psA, "psA"), (psA2, "psA2")])
        for hp in range(8):
            if sink:
                P.op("tensor", lambda e, hp=hp: e.matmul(
                    psO[:, 0:128], ones[0:1, :], bvrow[:, hp * 128:(hp + 1) * 128],
                    start=True, stop=True),
                    reads=["ones", "bvrow"], writes=["psO"])
                P.op("vector", lambda e: e.tensor_copy(bvb, psO[:, 0:128]),
                     reads=["psO"], writes=["bvb"])
                P.dma("sync", lambda e, hp=hp: e.dma_start(
                    bcol[:, 0:1], bq_d[hp * 128:(hp + 1) * 128, :]), writes=["bcol"])
                P.dma("sync", lambda e, hp=hp: e.dma_start(
                    bcol[:, 1:2], bk_d[hp * 128:(hp + 1) * 128, :]), writes=["bcol"])

            if not rope:
                for dst, rdst, wsb, wtag, bi in ((QT, "QT", wq, "wq", 0),
                                                 (KT, "KT", wk, "wk", 1)):
                    for nb in range(4):
                        pp, rpp = psalt()
                        for j in range(8):
                            P.op("tensor", lambda e, j=j, nb=nb, pp=pp, wsb=wsb, hp=hp:
                                 e.matmul(pp, wsb[j][:, hp * 128:(hp + 1) * 128],
                                          yT[j][:, nb * 512:(nb + 1) * 512],
                                          start=(j == 0), stop=(j == 7)),
                                 reads=[wtag + str(j), "yT%d" % j], writes=[rpp])
                        if qkvbias:
                            P.op("vector", lambda e, nb=nb, pp=pp, dst=dst, bi=bi:
                                 e.tensor_single_scalar(
                                     dst[:, nb * 512:(nb + 1) * 512], pp,
                                     bcol[:, bi:bi + 1], OPT.add),
                                 reads=[rpp, "bcol"], writes=[rdst])
                        else:
                            P.op("vector", lambda e, nb=nb, pp=pp, dst=dst:
                                 e.tensor_copy(dst[:, nb * 512:(nb + 1) * 512], pp),
                                 reads=[rpp], writes=[rdst])
            else:
                for dst, rdst, wsb, wtag in ((QT, "QT", wq, "wq"), (KT, "KT", wk, "wk")):
                    for tt in range(16):
                        pp, rpp = psalt()
                        qte = qtm[tt % 2]
                        rqt = "qtm%d" % (tt % 2)
                        for j in range(8):
                            P.op("tensor", lambda e, j=j, tt=tt, pp=pp, wsb=wsb, hp=hp:
                                 e.matmul(pp[:, 0:128],
                                          yT[j][:, tt * 128:(tt + 1) * 128],
                                          wsb[j][:, hp * 128:(hp + 1) * 128],
                                          start=(j == 0), stop=(j == 7)),
                                 reads=[wtag + str(j), "yT%d" % j], writes=[rpp])
                        P.op("scalar", lambda e, pp=pp, qte=qte: e.copy(qte, pp[:, 0:128]),
                             reads=[rpp], writes=[rqt])
                        for o in (0, 64):
                            P.op("vector", lambda e, qte=qte, o=o, tt=tt: e.tensor_mul(
                                t32[0], qte[:, o:o + 32], cosb[:, tt * 32:(tt + 1) * 32]),
                                reads=[rqt, "cosb"], writes=["t320"])
                            P.op("vector", lambda e, qte=qte, o=o, tt=tt: e.tensor_mul(
                                t32[1], qte[:, o + 32:o + 64], sinb[:, tt * 32:(tt + 1) * 32]),
                                reads=[rqt, "sinb"], writes=["t321"])
                            P.op("vector", lambda e, o=o: e.tensor_sub(
                                rof[:, o:o + 32], t32[0], t32[1]),
                                reads=["t320", "t321"], writes=["rof"])
                            P.op("vector", lambda e, qte=qte, o=o, tt=tt: e.tensor_mul(
                                t32[0], qte[:, o:o + 32], sinb[:, tt * 32:(tt + 1) * 32]),
                                reads=[rqt, "sinb"], writes=["t320"])
                            P.op("vector", lambda e, qte=qte, o=o, tt=tt: e.tensor_mul(
                                t32[1], qte[:, o + 32:o + 64], cosb[:, tt * 32:(tt + 1) * 32]),
                                reads=[rqt, "cosb"], writes=["t321"])
                            P.op("vector", lambda e, o=o: e.tensor_add(
                                rof[:, o + 32:o + 64], t32[0], t32[1]),
                                reads=["t320", "t321"], writes=["rof"])
                        P.op("scalar", lambda e: e.copy(robf, rof),
                             reads=["rof"], writes=["robf"])
                        pt = psT[tt % 2]
                        rpt = "psT%d" % (tt % 2)
                        P.op("tensor", lambda e, pt=pt: e.transpose(pt, robf, eye),
                             reads=["robf", "eye"], writes=[rpt])
                        P.op("vector", lambda e, tt=tt, pt=pt, dst=dst: e.tensor_copy(
                            dst[:, tt * 128:(tt + 1) * 128], pt),
                            reads=[rpt], writes=[rdst])

            for tt in range(16):
                pp, rpp = psalt()
                for j in range(8):
                    P.op("tensor", lambda e, j=j, tt=tt, pp=pp, hp=hp:
                         e.matmul(pp[:, 0:128], yT[j][:, tt * 128:(tt + 1) * 128],
                                  wv[j][:, hp * 128:(hp + 1) * 128],
                                  start=(j == 0), stop=(j == 7)),
                         reads=["wv" + str(j), "yT%d" % j], writes=[rpp])
                if sink:
                    P.op("vector", lambda e, tt=tt, pp=pp: e.tensor_add(
                        Vs[:, tt * 128:(tt + 1) * 128], pp[:, 0:128], bvb),
                        reads=[rpp, "bvb"], writes=["Vs"])
                else:
                    P.op("vector", lambda e, tt=tt, pp=pp: e.tensor_copy(
                        Vs[:, tt * 128:(tt + 1) * 128], pp[:, 0:128]),
                        reads=[rpp], writes=["Vs"])

            for hh in range(2):
                h = hp * 2 + hh
                o = hh * 64
                if t5:
                    for dl in range(16):
                        wn, rwn = win[dl % 2], "win%d" % (dl % 2)
                        off = h * 2304 + 128 * dl
                        P.dma("sync", lambda e, wn=wn, off=off: e.dma_start(
                            wn, AP(tab_h.tensor, off, [[1, 128], [1, 128]])),
                            reads=["tab_hr"], writes=[rwn])
                        P.op("tensor", lambda e, wn=wn: e.matmul(
                            psB, wn, jrev, start=True, stop=True),
                            reads=[rwn, "jrev"], writes=["psB"])
                        P.op("scalar", lambda e, dl=dl: e.copy(
                            bias_sb[:, dl * 128:(dl + 1) * 128], psB),
                            reads=["psB"], writes=["bias_sb"])

                for qt in range(16):
                    W = (qt + 1) * 128 if causal else T
                    nkb = W // 128
                    for nb in range((W + 511) // 512):
                        wcol = min(512, W - nb * 512)
                        pp, rpp = psalt()
                        P.op("tensor", lambda e, pp=pp, qt=qt, nb=nb, wcol=wcol, o=o:
                             e.matmul(pp[:, 0:wcol],
                                      QT[o:o + 64, qt * 128:(qt + 1) * 128],
                                      KT[o:o + 64, nb * 512:nb * 512 + wcol],
                                      start=True, stop=True),
                             reads=["QT", "KT"], writes=[rpp])
                        P.op("scalar", lambda e, pp=pp, nb=nb, wcol=wcol: e.mul(
                            S[:, nb * 512:nb * 512 + wcol], pp[:, 0:wcol], 0.125),
                            reads=[rpp], writes=["S"])
                    if t5:
                        for kb in range(qt + 1):
                            dl = qt - kb
                            P.op("vector", lambda e, kb=kb, dl=dl: e.tensor_add(
                                S[:, kb * 128:(kb + 1) * 128],
                                S[:, kb * 128:(kb + 1) * 128],
                                bias_sb[:, dl * 128:(dl + 1) * 128]),
                                reads=["S", "bias_sb"], writes=["S"])
                    if causal:
                        P.op("vector", lambda e, qt=qt: e.tensor_add(
                            S[:, qt * 128:(qt + 1) * 128],
                            S[:, qt * 128:(qt + 1) * 128], maskt),
                            reads=["S", "maskt"], writes=["S"])
                    P.op("vector", lambda e, W=W: e.reduce_max(stt[:, 0:1], S[:, 0:W], AX),
                         reads=["S"], writes=["stt"])
                    if sink:
                        P.op("vector", lambda e, h=h: e.tensor_max(
                            stt[:, 0:1], stt[:, 0:1], sinkb[:, h:h + 1]),
                            reads=["stt", "sinkb"], writes=["stt"])
                    P.op("vector", lambda e: e.tensor_scalar_mul(stt[:, 1:2], stt[:, 0:1], -1.0),
                         reads=["stt"], writes=["stt"])
                    P.op("scalar", lambda e, W=W: e.activation(
                        Pb[:, 0:W], S[:, 0:W], AF.Exp, bias=stt[:, 1:2], scale=1.0),
                        reads=["S", "stt"], writes=["Pb"])
                    P.op("vector", lambda e, W=W: e.reduce_sum(stt[:, 2:3], Pb[:, 0:W], AX),
                         reads=["Pb"], writes=["stt"])
                    if sink:
                        P.op("scalar", lambda e, h=h: e.activation(
                            stt[:, 4:5], sinkb[:, h:h + 1], AF.Exp, bias=stt[:, 1:2],
                            scale=1.0),
                            reads=["sinkb", "stt"], writes=["stt"])
                        P.op("vector", lambda e: e.tensor_add(
                            stt[:, 2:3], stt[:, 2:3], stt[:, 4:5]),
                            reads=["stt"], writes=["stt"])
                    P.op("vector", lambda e: e.reciprocal(stt[:, 3:4], stt[:, 2:3]),
                         reads=["stt"], writes=["stt"])
                    P.op("vector", lambda e, W=W: e.tensor_single_scalar(
                        Pb[:, 0:W], Pb[:, 0:W], stt[:, 3:4], OPT.mult),
                        reads=["Pb", "stt"], writes=["Pb"])
                    for kb in range(nkb):
                        pt, rpt = psT[kb % 2], "psT%d" % (kb % 2)
                        pts, rpts = PT[kb % 2], "PT%d" % (kb % 2)
                        P.op("tensor", lambda e, kb=kb, pt=pt: e.transpose(
                            pt, Pb[:, kb * 128:(kb + 1) * 128], eye),
                            reads=["Pb", "eye"], writes=[rpt])
                        P.op("scalar", lambda e, pt=pt, pts=pts: e.copy(pts, pt),
                             reads=[rpt], writes=[rpts])
                        P.op("tensor", lambda e, kb=kb, pts=pts, o=o, nkb=nkb: e.matmul(
                            psZ, Vs[:, kb * 128 + o:kb * 128 + o + 64], pts,
                            start=(kb == 0), stop=(kb == nkb - 1)),
                            reads=["Vs", rpts], writes=["psZ"])
                    j = h // 2
                    ro = (h % 2) * 64
                    P.op("vector", lambda e, j=j, ro=ro, qt=qt: e.tensor_copy(
                        zT[j][ro:ro + 64, qt * 128:(qt + 1) * 128], psZ),
                        reads=["psZ"], writes=["zT%d" % j])

        for tt in range(16):
            r0 = b * T + tt * 128
            for j in range(8):
                for c0 in (0, 512):
                    P.op("tensor", lambda e, j=j, tt=tt, c0=c0: e.matmul(
                        psO[:, c0:c0 + 512], zT[j][:, tt * 128:(tt + 1) * 128],
                        wo[j][:, c0:c0 + 512],
                        start=(j == 0), stop=(j == 7)),
                        reads=["zT%d" % j, "wo" + str(j)], writes=["psO"])
            P.dma("sync", lambda e, r0=r0: e.dma_start(xt, x_in[r0:r0 + 128, :]),
                  writes=["xt"])
            P.op("vector", lambda e: e.tensor_add(xm, psO, bob),
                 reads=["psO", "bob"], writes=["xm"])
            xo, rxo = xnew[tt % 2], "xnew%d" % (tt % 2)
            P.op("vector", lambda e, xo=xo: e.tensor_add(xo, xm, xt),
                 reads=["xm", "xt"], writes=[rxo])
            P.dma("gpsimd", lambda e, r0=r0, xo=xo: e.dma_start(
                x_out[r0:r0 + 128, :], xo),
                reads=[rxo], writes=["x_out"])

    P.final_wait()


def emit_moe(nc, P, d, sb, ps):
    E, FF = 8, 4096
    x_in, x_out = d["x_in"], d["x_out"]
    wrt_d, br_d = d["wrt"], d["br_row"]
    w1_d, b1_d, w2_d, b2_d = d["w1t"], d["b1c"], d["w2t"], d["b2_row"]
    gam_d, bet_d = d["gam_row"], d["bet_row"]
    ones_d, eye_d = d["c_ones"], d["c_eye_bf"]
    ytb_h, acc_h = d["ytb_h"], d["acc_h"]

    w1t = [sb([128, FF], BF16) for _ in range(8)]
    w2t = [sb([128, D], BF16) for _ in range(32)]
    TG = 256
    hT = sb([128, 32 * TG], BF16)
    yTg = sb([128, 8 * TG], BF16)
    acc = sb([128, D], F32)
    rw = sb([128, 32 * 8], F32)
    stg = [sb([128, 512], F32) for _ in range(2)]
    xt = sb([128, D], F32)
    xm = sb([128, D], F32)
    yt = sb([128, D], BF16)
    ytb = sb([128, 8 * 128], BF16)
    xnew = [sb([128, D], F32)]
    gamb = sb([128, D], BF16)
    betb = sb([128, D], BF16)
    b2b = sb([128, D], BF16)
    brb = sb([128, E], F32)
    wrtb = [sb([128, E], BF16) for _ in range(8)]
    b1c = sb([128, 32], F32)
    lg = sb([128, E], F32)
    eq1 = sb([128, E], F32)
    lg2 = sb([128, E], F32)
    eq2 = sb([128, E], F32)
    rwt = sb([128, E], F32)
    stt = sb([128, 8], F32)
    ones = sb([1, 128], F32)
    onesb = sb([1, 128], BF16)
    eye = sb([128, 128], BF16)
    grow = sb([1, D], F32)
    growb = sb([1, D], BF16)

    ps1 = [ps([128, 512], F32) for _ in range(2)]
    ps2 = ps([128, 1024], F32)
    tmp2 = ps([128, 1024], F32)
    psL = ps([128, E], F32)
    psT = ps([128, 128], BF16)

    P.dma("sync", lambda e: e.dma_start(ones, ones_d), writes=["ones"])
    P.dma("sync", lambda e: e.dma_start(eye, eye_d), writes=["eye"])
    P.op("scalar", lambda e: e.copy(onesb, ones), reads=["ones"], writes=["onesb"])
    for j in range(8):
        P.dma("sync", lambda e, j=j: e.dma_start(
            stg[0][:, 0:E], wrt_d[j * 128:(j + 1) * 128, :]), writes=["stg0"])
        P.op("scalar", lambda e, j=j: e.copy(wrtb[j], stg[0][:, 0:E]),
             reads=["stg0"], writes=["wrtb%d" % j])
    P.dma("sync", lambda e: e.dma_start(grow[:, 0:E], br_d), writes=["grow"])
    P.op("tensor", lambda e: e.matmul(psL, ones[0:1, :], grow[:, 0:E],
                                      start=True, stop=True),
         reads=["ones", "grow"], writes=["psL"])
    P.op("vector", lambda e: e.tensor_copy(brb, psL), reads=["psL"], writes=["brb"])
    for src, dst, rdst in ((gam_d, gamb, "gamb"), (bet_d, betb, "betb")):
        P.dma("sync", lambda e, src=src: e.dma_start(grow, src), writes=["grow"])
        P.op("scalar", lambda e: e.copy(growb, grow), reads=["grow"], writes=["growb"])
        for c0 in (0, 512):
            P.op("tensor", lambda e, c0=c0: e.matmul(
                ps2[:, c0:c0 + 512], onesb[0:1, :], growb[:, c0:c0 + 512],
                start=True, stop=True),
                reads=["onesb", "growb"], writes=["ps2"])
        P.op("vector", lambda e, dst=dst: e.tensor_copy(dst, ps2),
             reads=["ps2"], writes=[rdst])

    for ti in range(32):
        r0 = ti * 128
        P.dma("sync", lambda e, r0=r0: e.dma_start(xt, x_in[r0:r0 + 128, :]),
              writes=["xt"])
        emit_ln(P, xt, "xt", xm, "xm", stt, "stt")
        P.dma("gpsimd", lambda e, r0=r0: e.dma_start(acc_h[r0:r0 + 128, :], xm),
              reads=["xm"], writes=["acc_h"])
        P.op("scalar", lambda e: e.copy(yt, xm), reads=["xm"], writes=["yt"])
        for j in range(8):
            P.op("tensor", lambda e, j=j: e.transpose(
                psT, yt[:, j * 128:(j + 1) * 128], eye),
                reads=["yt", "eye"], writes=["psT"])
            P.op("vector", lambda e, j=j: e.tensor_copy(
                ytb[:, j * 128:(j + 1) * 128], psT),
                reads=["psT"], writes=["ytb"])
            P.dma("gpsimd", lambda e, j=j, ti=ti: e.dma_start(
                ytb_h[j * 128:(j + 1) * 128, ti * 128:(ti + 1) * 128],
                ytb[:, j * 128:(j + 1) * 128]),
                reads=["ytb"], writes=["ytb_h"])
        for j in range(8):
            P.op("tensor", lambda e, j=j: e.matmul(
                psL, ytb[:, j * 128:(j + 1) * 128], wrtb[j],
                start=(j == 0), stop=(j == 7)),
                reads=["ytb", "wrtb%d" % j], writes=["psL"])
        P.op("vector", lambda e: e.tensor_add(lg, psL, brb),
             reads=["psL", "brb"], writes=["lg"])
        P.op("vector", lambda e: e.reduce_max(stt[:, 0:1], lg, AX),
             reads=["lg"], writes=["stt"])
        P.op("vector", lambda e: e.tensor_single_scalar(
            eq1, lg, stt[:, 0:1], OPT.is_equal), reads=["lg", "stt"], writes=["eq1"])
        P.op("vector", lambda e: e.scalar_tensor_tensor(
            lg2, eq1, NEG, lg, OPT.mult, OPT.add),
            reads=["eq1", "lg"], writes=["lg2"])
        P.op("vector", lambda e: e.reduce_max(stt[:, 1:2], lg2, AX),
             reads=["lg2"], writes=["stt"])
        P.op("vector", lambda e: e.tensor_single_scalar(
            eq2, lg2, stt[:, 1:2], OPT.is_equal), reads=["lg2", "stt"], writes=["eq2"])
        P.op("vector", lambda e: e.tensor_sub(stt[:, 2:3], stt[:, 1:2], stt[:, 0:1]),
             reads=["stt"], writes=["stt"])
        P.op("scalar", lambda e: e.activation(stt[:, 3:4], stt[:, 2:3], AF.Exp),
             reads=["stt"], writes=["stt"])
        P.op("vector", lambda e: e.tensor_scalar_add(stt[:, 4:5], stt[:, 3:4], 1.0),
             reads=["stt"], writes=["stt"])
        P.op("vector", lambda e: e.reciprocal(stt[:, 5:6], stt[:, 4:5]),
             reads=["stt"], writes=["stt"])
        P.op("vector", lambda e: e.tensor_scalar(
            stt[:, 6:7], stt[:, 5:6], -1.0, 1.0, OPT.mult, OPT.add),
            reads=["stt"], writes=["stt"])
        P.op("vector", lambda e: e.tensor_single_scalar(
            rwt, eq1, stt[:, 5:6], OPT.mult), reads=["eq1", "stt"], writes=["rwt"])
        P.op("vector", lambda e, ti=ti: e.scalar_tensor_tensor(
            rw[:, ti * 8:(ti + 1) * 8], eq2, stt[:, 6:7], rwt, OPT.mult, OPT.add),
            reads=["eq2", "stt", "rwt"], writes=["rw"])

    for ee in range(E):
        emit_load_bf16(P, w1_d[ee], w1t, stg, ["stg0", "stg1"], "w1t", FF, chunk=512)
        emit_load_bf16(P, w2_d[ee], w2t, stg, ["stg0", "stg1"], "w2t", D, chunk=512)
        for ft in range(32):
            P.dma("sync", lambda e, ee=ee, ft=ft: e.dma_start(
                b1c[:, ft:ft + 1], b1_d[ee, ft * 128:(ft + 1) * 128, :]),
                writes=["b1c"])
        P.dma("sync", lambda e, ee=ee: e.dma_start(grow, b2_d[ee]), writes=["grow"])
        P.op("scalar", lambda e: e.copy(growb, grow), reads=["grow"], writes=["growb"])
        for c0 in (0, 512):
            P.op("tensor", lambda e, c0=c0: e.matmul(
                tmp2[:, c0:c0 + 512], onesb[0:1, :], growb[:, c0:c0 + 512],
                start=True, stop=True),
                reads=["onesb", "growb"], writes=["tmp2"])
        P.op("vector", lambda e: e.tensor_copy(b2b, tmp2),
             reads=["tmp2"], writes=["b2b"])

        for g in range(16):
            for j in range(8):
                P.dma("sync", lambda e, j=j, g=g: e.dma_start(
                    yTg[:, j * TG:(j + 1) * TG],
                    ytb_h[j * 128:(j + 1) * 128, g * TG:(g + 1) * TG]),
                    reads=["ytb_h"], writes=["yTg"])
            for ft in range(32):
                pp, rpp = ps1[ft % 2], "ps1%d" % (ft % 2)
                for j in range(8):
                    P.op("tensor", lambda e, j=j, ft=ft, pp=pp: e.matmul(
                        pp[:, 0:TG], w1t[j][:, ft * 128:(ft + 1) * 128],
                        yTg[:, j * TG:(j + 1) * TG],
                        start=(j == 0), stop=(j == 7)),
                        reads=["w1t%d" % j, "yTg"], writes=[rpp])
                P.op("scalar", lambda e, ft=ft, pp=pp: e.activation(
                    hT[:, ft * TG:(ft + 1) * TG], pp[:, 0:TG], AF.Relu,
                    bias=b1c[:, ft:ft + 1], scale=1.0),
                    reads=[rpp, "b1c"], writes=["hT"])
            for tt in range(2):
                ti = g * 2 + tt
                r0 = ti * 128
                for ft in range(32):
                    for c0 in (0, 512):
                        P.op("tensor", lambda e, ft=ft, tt=tt, c0=c0: e.matmul(
                            ps2[:, c0:c0 + 512],
                            hT[:, ft * TG + tt * 128:ft * TG + (tt + 1) * 128],
                            w2t[ft][:, c0:c0 + 512],
                            start=(ft == 0), stop=(ft == 31)),
                            reads=["hT", "w2t%d" % ft], writes=["ps2"])
                P.dma("sync", lambda e, r0=r0: e.dma_start(acc, acc_h[r0:r0 + 128, :]),
                      reads=["acc_h"], writes=["acc"])
                P.op("vector", lambda e: e.tensor_add(tmp2, ps2, b2b),
                     reads=["ps2", "b2b"], writes=["tmp2"])
                P.op("vector", lambda e, ti=ti, ee=ee: e.scalar_tensor_tensor(
                    acc, tmp2, rw[:, ti * 8 + ee:ti * 8 + ee + 1], acc,
                    OPT.mult, OPT.add),
                    reads=["tmp2", "rw", "acc"], writes=["acc"])
                P.dma("gpsimd", lambda e, r0=r0: e.dma_start(
                    acc_h[r0:r0 + 128, :], acc),
                    reads=["acc"], writes=["acc_h"])

    for ti in range(32):
        r0 = ti * 128
        P.dma("sync", lambda e, r0=r0: e.dma_start(xt, acc_h[r0:r0 + 128, :]),
              reads=["acc_h"], writes=["xt"])
        emit_ln(P, xt, "xt", xm, "xm", stt, "stt", gb=gamb, bb=betb,
                rgb="gamb", rbb="betb")
        P.dma("sync", lambda e, r0=r0: e.dma_start(acc, x_in[r0:r0 + 128, :]),
              writes=["acc"])
        xo, rxo = xnew[0], "xnew0"
        P.op("vector", lambda e, xo=xo: e.tensor_add(xo, xm, acc),
             reads=["xm", "acc"], writes=[rxo])
        P.dma("gpsimd", lambda e, r0=r0, xo=xo: e.dma_start(
            x_out[r0:r0 + 128, :], xo),
            reads=[rxo], writes=["x_out"])

    P.final_wait()


def emit_rwkv(nc, P, d, sb, ps):
    x_in, x_out = d["x_in"], d["x_out"]
    wr_d, wk_d, wv_d = d["wrt"], d["wkt"], d["wvt"]
    mur_d, muk_d = d["mur_col"], d["muk_col"]
    eye_d = d["c_eye_bf"]

    buf = [sb([128, T + 1], BF16) for _ in range(8)]
    xrT = [sb([128, T], BF16) for _ in range(8)]
    xkT = [sb([128, T], BF16) for _ in range(8)]
    wrt = [sb([128, D], BF16) for _ in range(8)]
    wkt = [sb([128, D], BF16) for _ in range(8)]
    wvt = [sb([128, D], BF16) for _ in range(8)]
    dbuf = sb([128, T], F32)
    k2T = sb([128, D], BF16)
    rsb = sb([128, D], F32)
    krel = sb([128, D], F32)
    k2b = sb([128, D], BF16)
    stg = [sb([128, 1024], F32) for _ in range(2)]
    xt = sb([128, D], F32)
    xm = sb([128, D], F32)
    yt = sb([128, D], BF16)
    xnew = [sb([128, D], F32) for _ in range(2)]
    muc = sb([128, 2], F32)
    stt = sb([128, 8], F32)
    eye = sb([128, 128], BF16)

    psR = ps([128, 1024], F32)
    psK = ps([128, 1024], F32)
    psO = ps([128, 1024], F32)
    psT = [ps([128, 128], BF16) for _ in range(2)]

    P.dma("sync", lambda e: e.dma_start(eye, eye_d), writes=["eye"])
    emit_load_bf16(P, wr_d, wrt, stg, ["stg0", "stg1"], "wrt", D)
    emit_load_bf16(P, wk_d, wkt, stg, ["stg0", "stg1"], "wkt", D)
    emit_load_bf16(P, wv_d, wvt, stg, ["stg0", "stg1"], "wvt", D)

    for b in range(2):
        for j in range(8):
            P.op("vector", lambda e, j=j: e.memset(buf[j][:, 0:1], 0.0),
                 reads=[], writes=["buf%d" % j])
        for tt in range(16):
            r0 = b * T + tt * 128
            P.dma("sync", lambda e, r0=r0: e.dma_start(xt, x_in[r0:r0 + 128, :]),
                  writes=["xt"])
            emit_ln(P, xt, "xt", xm, "xm", stt, "stt")
            P.op("scalar", lambda e: e.copy(yt, xm), reads=["xm"], writes=["yt"])
            for j in range(8):
                pt, rpt = psT[j % 2], "psT%d" % (j % 2)
                P.op("tensor", lambda e, j=j, pt=pt: e.transpose(
                    pt, yt[:, j * 128:(j + 1) * 128], eye),
                    reads=["yt", "eye"], writes=[rpt])
                P.op("vector", lambda e, j=j, tt=tt, pt=pt: e.tensor_copy(
                    buf[j][:, 1 + tt * 128:1 + (tt + 1) * 128], pt),
                    reads=[rpt], writes=["buf%d" % j])
        for j in range(8):
            P.dma("sync", lambda e, j=j: e.dma_start(
                muc[:, 0:1], mur_d[j * 128:(j + 1) * 128, :]), writes=["muc"])
            P.dma("sync", lambda e, j=j: e.dma_start(
                muc[:, 1:2], muk_d[j * 128:(j + 1) * 128, :]), writes=["muc"])
            P.op("vector", lambda e, j=j: e.tensor_sub(
                dbuf, buf[j][:, 1:T + 1], buf[j][:, 0:T]),
                reads=["buf%d" % j], writes=["dbuf"])
            P.op("vector", lambda e, j=j: e.scalar_tensor_tensor(
                xrT[j], dbuf, muc[:, 0:1], buf[j][:, 0:T], OPT.mult, OPT.add),
                reads=["dbuf", "muc", "buf%d" % j], writes=["xrT%d" % j])
            P.op("vector", lambda e, j=j: e.scalar_tensor_tensor(
                xkT[j], dbuf, muc[:, 1:2], buf[j][:, 0:T], OPT.mult, OPT.add),
                reads=["dbuf", "muc", "buf%d" % j], writes=["xkT%d" % j])
        for tt in range(16):
            r0 = b * T + tt * 128
            for j in range(8):
                for c0 in (0, 512):
                    P.op("tensor", lambda e, j=j, tt=tt, c0=c0: e.matmul(
                        psR[:, c0:c0 + 512], xrT[j][:, tt * 128:(tt + 1) * 128],
                        wrt[j][:, c0:c0 + 512],
                        start=(j == 0), stop=(j == 7)),
                        reads=["xrT%d" % j, "wrt%d" % j], writes=["psR"])
            P.op("scalar", lambda e: e.activation(rsb, psR, AF.Sigmoid),
                 reads=["psR"], writes=["rsb"])
            for j in range(8):
                for c0 in (0, 512):
                    P.op("tensor", lambda e, j=j, tt=tt, c0=c0: e.matmul(
                        psK[:, c0:c0 + 512], xkT[j][:, tt * 128:(tt + 1) * 128],
                        wkt[j][:, c0:c0 + 512],
                        start=(j == 0), stop=(j == 7)),
                        reads=["xkT%d" % j, "wkt%d" % j], writes=["psK"])
            P.op("scalar", lambda e: e.activation(krel, psK, AF.Relu),
                 reads=["psK"], writes=["krel"])
            P.op("scalar", lambda e: e.square(k2b, krel),
                 reads=["krel"], writes=["k2b"])
            for j in range(8):
                pt, rpt = psT[j % 2], "psT%d" % (j % 2)
                P.op("tensor", lambda e, j=j, pt=pt: e.transpose(
                    pt, k2b[:, j * 128:(j + 1) * 128], eye),
                    reads=["k2b", "eye"], writes=[rpt])
                P.op("vector", lambda e, j=j, pt=pt: e.tensor_copy(
                    k2T[:, j * 128:(j + 1) * 128], pt),
                    reads=[rpt], writes=["k2T"])
            for j in range(8):
                for c0 in (0, 512):
                    P.op("tensor", lambda e, j=j, c0=c0: e.matmul(
                        psO[:, c0:c0 + 512], k2T[:, j * 128:(j + 1) * 128],
                        wvt[j][:, c0:c0 + 512],
                        start=(j == 0), stop=(j == 7)),
                        reads=["k2T", "wvt%d" % j], writes=["psO"])
            P.dma("sync", lambda e, r0=r0: e.dma_start(xt, x_in[r0:r0 + 128, :]),
                  writes=["xt"])
            P.op("vector", lambda e: e.tensor_mul(xm, psO, rsb),
                 reads=["psO", "rsb"], writes=["xm"])
            xo, rxo = xnew[tt % 2], "xnew%d" % (tt % 2)
            P.op("vector", lambda e, xo=xo: e.tensor_add(xo, xm, xt),
                 reads=["xm", "xt"], writes=[rxo])
            P.dma("gpsimd", lambda e, r0=r0, xo=xo: e.dma_start(
                x_out[r0:r0 + 128, :], xo),
                reads=[rxo], writes=["x_out"])

    P.final_wait()


EMITTERS = {"l0a": lambda nc, P, d, sb, ps: emit_attn(nc, P, d, sb, ps, "l0a"),
            "l0b": lambda nc, P, d, sb, ps: emit_attn(nc, P, d, sb, ps, "l0b"),
            "l1a": lambda nc, P, d, sb, ps: emit_attn(nc, P, d, sb, ps, "l1a"),
            "l1b": lambda nc, P, d, sb, ps: emit_attn(nc, P, d, sb, ps, "l1b"),
            "moe": emit_moe, "rwkv": emit_rwkv}

ATTN_W = {"l0a": ["wq", "wk", "wv", "wo", "bo_row", "bq_col", "bk_col", "bv_row",
                  "sink_row"],
          "l0b": ["wq", "wk", "wv", "wo", "bo_row"],
          "l1a": ["wq", "wk", "wv", "wo", "bo_row"],
          "l1b": ["wq", "wk", "wv", "wo", "bo_row", "relemb"]}
W_SHAPES = {"wq": [D, D], "wk": [D, D], "wv": [D, D], "wo": [D, D],
            "bo_row": [1, D], "bq_col": [D, 1], "bk_col": [D, 1], "bv_row": [1, D],
            "sink_row": [1, NH], "relemb": [32, NH],
            "wkt": [D, D], "wvt": [D, D],
            "mur_col": [D, 1], "muk_col": [D, 1],
            "br_row": [1, 8], "w1t": [8, D, 4096], "b1c": [8, 4096, 1],
            "w2t": [8, 4096, D], "b2_row": [8, 1, D], "gam_row": [1, D],
            "bet_row": [1, D]}
MOE_W = ["wrt", "br_row", "w1t", "b1c", "w2t", "b2_row", "gam_row", "bet_row"]
RWKV_W = ["wrt", "wkt", "wvt", "mur_col", "muk_col"]

PLAN = [("l0a", "l0a"), ("moe", "moe"), ("l0b", "l0b"), ("moe", "moe"),
        ("l1a", "l1a"), ("rwkv", "rwkv"), ("l1b", "l1b"), ("rwkv", "rwkv")]

CONST_DECL = {"c_ones": ([1, 128], F32), "c_eye_bf": ([128, 128], BF16),
              "c_mask": ([128, 128], F32), "cos_tm": ([T, 32], F32),
              "sin_tm": ([T, 32], F32), "oh_bf": ([32, 2304], BF16),
              "j_bf": ([128, 128], BF16)}


def _decl_weight_aps(nc, pfx, names):
    out = {}
    for n in names:
        if n == "wrt":
            shp = [D, 8] if pfx == "moe" else [D, D]
        else:
            shp = W_SHAPES[n]
        out[n] = nc.dram_tensor(pfx + "_" + n, shp, F32, kind="ExternalInput").ap()
    return out


def _phase_sbps(nc, st_, nm):
    def sbf(shape, dt):
        nm[0] += 1
        return st_.enter_context(nc.sbuf_tensor("sb%d" % nm[0], shape, dt)).ap()

    def psf(shape, dt):
        nm[0] += 1
        return st_.enter_context(nc.psum_tensor("ps%d" % nm[0], shape, dt)).ap()
    return sbf, psf


def build_fused():
    nc = bass.Bass("TRN2", target_bir_lowering=False)
    P = Pr()
    with contextlib.ExitStack() as outer:
        consts = {k: nc.dram_tensor(k, s, dt, kind="ExternalInput").ap()
                  for k, (s, dt) in CONST_DECL.items()}
        x0 = nc.dram_tensor("x_in", [NT, D], F32, kind="ExternalInput").ap()
        xF = nc.dram_tensor("x_out", [NT, D], F32, kind="ExternalOutput").ap()
        xh = [nc.dram_tensor("xh%d" % k, [NT, D], F32, kind="Internal").ap()
              for k in range(7)]
        wsets = {}
        for kind in ("l0a", "l0b", "l1a", "l1b"):
            wsets[kind] = _decl_weight_aps(nc, kind, ATTN_W[kind])
        wsets["moe"] = _decl_weight_aps(nc, "moe", MOE_W)
        wsets["rwkv"] = _decl_weight_aps(nc, "rwkv", RWKV_W)
        intern = {
            "moe": {"ytb_h": nc.dram_tensor("ytb_h", [D, NT], BF16,
                                            kind="Internal").ap(),
                    "acc_h": nc.dram_tensor("acc_h", [NT, D], F32,
                                            kind="Internal").ap()},
            "l1b": {"tab_h": nc.dram_tensor("tab_h", [NH, 2304], BF16,
                                            kind="Internal").ap()},
        }
        nm = [0]
        for k, (kind, wkey) in enumerate(PLAN):
            P.new_phase()
            d = dict(consts)
            d.update(wsets[wkey])
            d.update(intern.get(kind, {}))
            d["x_in"] = x0 if k == 0 else xh[k - 1]
            d["x_out"] = xF if k == len(PLAN) - 1 else xh[k]
            with contextlib.ExitStack() as pst:
                sbf, psf = _phase_sbps(nc, pst, nm)
                EMITTERS[kind](nc, P, d, sbf, psf)
                P.finalize_phase(nc, outer)
    return nc


def build_single(kind):
    nc = bass.Bass("TRN2", target_bir_lowering=False)
    P = Pr()
    with contextlib.ExitStack() as outer:
        consts = {k: nc.dram_tensor(k, s, dt, kind="ExternalInput").ap()
                  for k, (s, dt) in CONST_DECL.items()}
        x0 = nc.dram_tensor("x_in", [NT, D], F32, kind="ExternalInput").ap()
        xF = nc.dram_tensor("x_out", [NT, D], F32, kind="ExternalOutput").ap()
        if kind in ATTN_W:
            ws = _decl_weight_aps(nc, kind, ATTN_W[kind])
        elif kind == "moe":
            ws = _decl_weight_aps(nc, "moe", MOE_W)
        else:
            ws = _decl_weight_aps(nc, "rwkv", RWKV_W)
        d = dict(consts)
        d.update(ws)
        if kind == "moe":
            d["ytb_h"] = nc.dram_tensor("ytb_h", [D, NT], BF16, kind="Internal").ap()
            d["acc_h"] = nc.dram_tensor("acc_h", [NT, D], F32, kind="Internal").ap()
        if kind == "l1b":
            d["tab_h"] = nc.dram_tensor("tab_h", [NH, 2304], BF16, kind="Internal").ap()
        d["x_in"] = x0
        d["x_out"] = xF
        P.new_phase()
        nm = [0]
        with contextlib.ExitStack() as pst:
            sbf, psf = _phase_sbps(nc, pst, nm)
            EMITTERS[kind](nc, P, d, sbf, psf)
            P.finalize_phase(nc, outer)
    return nc


_NC_CACHE = {}


def _get_nc(kind):
    if kind not in _NC_CACHE:
        _NC_CACHE[kind] = build_fused() if kind == "fused" else build_single(kind)
    return _NC_CACHE[kind]


def _consts():
    c = {}
    c["c_ones"] = np.ones((1, 128), np.float32)
    c["c_eye_bf"] = np.eye(128).astype(BF)
    c["c_mask"] = np.triu(np.full((128, 128), NEG, np.float32), k=1)
    half = 32
    theta = 1.0 / 10000.0 ** (np.arange(half, dtype=np.float32) / half)
    ang = np.arange(T, dtype=np.float32)[:, None] * theta[None, :]
    c["cos_tm"] = np.cos(ang).astype(np.float32)
    c["sin_tm"] = np.sin(ang).astype(np.float32)
    oh = np.zeros((32, 2304), np.float32)
    for u in range(2304):
        n = max(u - 127, 0)
        oh[_bucket(n), u] = 1.0
    c["oh_bf"] = oh.astype(BF)
    c["j_bf"] = np.flipud(np.eye(128)).astype(BF)
    return c


def _wmaps(ii):
    ca = np.ascontiguousarray
    wqkv0 = ii["L0_attn_a_W_QKV"]
    wb = ii["L0_attn_b_w_qkv"]
    wa1 = ii["L1_attn_a_w_qkv"]
    m = {}
    m["l0a"] = {
        "wq": ca(wqkv0[:, 0:D]), "wk": ca(wqkv0[:, D:2 * D]),
        "wv": ca(wqkv0[:, 2 * D:3 * D]), "wo": ca(ii["L0_attn_a_W_O"]),
        "bo_row": ca(ii["L0_attn_a_b_O"][None, :]),
        "bq_col": ca(ii["L0_attn_a_b_QKV"][0:D, None]),
        "bk_col": ca(ii["L0_attn_a_b_QKV"][D:2 * D, None]),
        "bv_row": ca(ii["L0_attn_a_b_QKV"][None, 2 * D:3 * D]),
        "sink_row": ca(ii["L0_attn_a_sink_logit"][None, :]),
    }
    m["l0b"] = {
        "wq": ca(wb[:, 0:64, :].transpose(2, 0, 1)).reshape(D, D),
        "wk": ca(wb[:, 64:128, :].transpose(2, 0, 1)).reshape(D, D),
        "wv": ca(wb[:, 128:192, :].transpose(2, 0, 1)).reshape(D, D),
        "wo": ca(ii["L0_attn_b_w_proj"].T),
        "bo_row": ca(ii["L0_attn_b_b_proj"][None, :]),
    }
    m["moe"] = {
        "wrt": ca(ii["L0_ffn_Wr"].T), "br_row": ca(ii["L0_ffn_br"][None, :]),
        "w1t": ca(ii["L0_ffn_W1"].transpose(0, 2, 1)),
        "b1c": ca(ii["L0_ffn_b1"][:, :, None]),
        "w2t": ca(ii["L0_ffn_W2"].transpose(0, 2, 1)),
        "b2_row": ca(ii["L0_ffn_b2"][:, None, :]),
        "gam_row": ca(ii["L0_ffn_gamma"][None, :]),
        "bet_row": ca(ii["L0_ffn_beta"][None, :]),
    }
    m["l1a"] = {
        "wq": ca(wa1[:, 0:D]), "wk": ca(wa1[:, D:2 * D]), "wv": ca(wa1[:, 2 * D:3 * D]),
        "wo": ca(ii["L1_attn_a_w_out"]),
        "bo_row": ca(ii["L1_attn_a_b_out"][None, :]),
    }
    m["l1b"] = {
        "wq": ca(ii["L1_attn_b_Wq"]), "wk": ca(ii["L1_attn_b_Wk"]),
        "wv": ca(ii["L1_attn_b_Wv"]), "wo": ca(ii["L1_attn_b_Wo"]),
        "bo_row": ca(ii["L1_attn_b_bo"][None, :]),
        "relemb": ca(ii["L1_attn_b_rel_emb"]),
    }
    m["rwkv"] = {
        "wrt": ca(ii["L1_ffn_W_r"].T), "wkt": ca(ii["L1_ffn_W_k"].T),
        "wvt": ca(ii["L1_ffn_W_v"].T),
        "mur_col": ca(ii["L1_ffn_mu_r"][:, None]),
        "muk_col": ca(ii["L1_ffn_mu_k"][:, None]),
    }
    return m


def kernel(**inputs):
    ii = {k: np.asarray(v) for k, v in inputs.items()}
    C = _consts()
    m = _wmaps(ii)
    x = ii["L0_attn_a_x"].reshape(8, 2, T, D)
    shards = [np.ascontiguousarray(x[i]).reshape(NT, D) for i in range(8)]

    fused_map = dict(C)
    for kind, wm in m.items():
        for n, a in wm.items():
            fused_map[kind + "_" + n] = a

    try:
        nc = _get_nc("fused")
        in_maps = [dict(fused_map, x_in=shards[i]) for i in range(8)]
        res = run_bass_kernel_spmd(nc, in_maps, core_ids=list(range(8)))
        cur = [np.asarray(r["x_out"]) for r in res.results]
    except Exception:
        import traceback
        traceback.print_exc()
        cur = shards
        for kind, wkey in PLAN:
            nc = _get_nc(kind)
            wm = dict(C)
            for n, a in m[wkey].items():
                wm[wkey + "_" + n] = a
            in_maps = [dict(wm, x_in=cur[i]) for i in range(8)]
            res = run_bass_kernel_spmd(nc, in_maps, core_ids=list(range(8)))
            cur = [np.asarray(r["x_out"]) for r in res.results]
    return np.concatenate([c.reshape(2, T, D) for c in cur], axis=0)
